# version 05
# baseline (speedup 1.0000x reference)
import contextlib
import os
import numpy as np
import ml_dtypes
import concourse.bass as bass
import concourse.mybir as mybir
from concourse.bass_utils import run_bass_kernel_spmd

F32 = mybir.dt.float32
BF16 = mybir.dt.bfloat16
I32 = mybir.dt.int32
AF = mybir.ActivationFunctionType
ALU = mybir.AluOpType
AX = mybir.AxisListType

H = 1024
L = 8192
DI = 2048
DS = 16
T = 512
NT = L // T
KH = H // 128
KD = DI // 128
NQ = 2
NSC = DS // NQ
BL = T + 1
RC = 12582912.0
NCORE = 8


def build_nc():
    ph = int(os.environ.get("KPH", "7"))
    su = int(os.environ.get("KSU", "4"))
    nc = bass.Bass("TRN2", target_bir_lowering=False,
                   detect_race_conditions=False)

    xin = nc.dram_tensor("xin", [H, L], F32, kind="ExternalInput")
    w_in = nc.dram_tensor("w_in", [H, 2 * DI], F32, kind="ExternalInput")
    w_xp = nc.dram_tensor("w_xp", [DI, 2 * DS + DI], F32, kind="ExternalInput")
    w_ou = nc.dram_tensor("w_ou", [DI, H], F32, kind="ExternalInput")
    wmod = nc.dram_tensor("wmod", [H, 6 * H], F32, kind="ExternalInput")
    alog = nc.dram_tensor("alog", [DI, DS], F32, kind="ExternalInput")
    dprm = nc.dram_tensor("dprm", [DI], F32, kind="ExternalInput")
    cnvw = nc.dram_tensor("cnvw", [DI, 4], F32, kind="ExternalInput")
    cnvb = nc.dram_tensor("cnvb", [DI], F32, kind="ExternalInput")
    nrmw = nc.dram_tensor("nrmw", [H], F32, kind="ExternalInput")
    fbia = nc.dram_tensor("fbia", [H], F32, kind="ExternalInput")
    crow = nc.dram_tensor("crow", [H], F32, kind="ExternalInput")
    wsca = nc.dram_tensor("wsca", [1, 1], F32, kind="ExternalInput")
    pack = nc.dram_tensor("pack", [256, 1024], I32, kind="ExternalInput")
    idbf = nc.dram_tensor("idbf", [128, 128], BF16, kind="ExternalInput")
    onbf = nc.dram_tensor("onbf", [128, 1], BF16, kind="ExternalInput")
    onf3 = nc.dram_tensor("onf3", [128, 1], F32, kind="ExternalInput")
    out = nc.dram_tensor("out", [H, L], F32, kind="ExternalOutput")
    dbg = nc.dram_tensor("dbg", [6, 128, KD * T], F32, kind="ExternalOutput")
    dbh = nc.dram_tensor("dbh", [6, 128, KD * T], BF16, kind="ExternalOutput")

    wi_s = nc.dram_tensor("wi_s", [8 * KH * 128 * 512], BF16)
    wxbc_s = nc.dram_tensor("wxbc_s", [KD * 128 * 32], BF16)
    wxd_s = nc.dram_tensor("wxd_s", [16 * KD * 128 * 128], BF16)
    wo_s = nc.dram_tensor("wo_s", [8 * KD * 128 * 128], BF16)
    wf_s = nc.dram_tensor("wf_s", [4 * KH * 128 * 256], BF16)
    rows1 = nc.dram_tensor("rows1", [2, T], BF16)
    rows2 = nc.dram_tensor("rows2", [1, T], BF16)
    rows3 = nc.dram_tensor("rows3", [2, T], BF16)
    bcscr = nc.dram_tensor("bcscr", [2 * DS, T], BF16)
    qsscr = nc.dram_tensor("qsscr", [T], F32)
    qiscr = nc.dram_tensor("qiscr", [T], F32)

    es = contextlib.ExitStack()
    with es:
        def sb(name, shape, dt=BF16):
            return es.enter_context(nc.sbuf_tensor(name, shape, dt))

        def psum(name, shape, dt=F32):
            return es.enter_context(nc.psum_tensor(name, shape, dt))

        xf = sb("xf", [128, KH * T], F32)
        bsq = sb("bsq", [128, 2 * T])
        hln = sb("hln", [128, KH * T])
        h1 = sb("h1", [128, KH * T])
        xss = sb("xss", [128, T + 3])
        halo = sb("halo", [128, KD * 3])
        cacc = sb("cacc", [128, T])
        xcv = sb("xcv", [128, KD * T])
        sz = sb("sz", [128, KD * T])
        scrA = sb("scrA", [128, 2 * T], F32)
        scrB = sb("scrB", [128, T], F32)
        dlt = sb("dlt", [128, 2 * T])
        dxs = sb("dxs", [128, T])
        Brep = sb("Brep", [128, DS * T])
        Crep = sb("Crep", [128, DS * T])
        bc16 = sb("bc16", [32, T])
        dAb = [sb(f"dA{j}", [128, NQ * BL], F32) for j in range(2)]
        ub = sb("ub", [128, NQ * BL])
        hb = sb("hb", [128, NQ * BL])
        yac = sb("yac", [128, T], F32)
        stt = sb("stt", [128, KD * DS], F32)
        vsc = sb("vsc", [128, T], F32)
        xnw = sb("xnw", [128, KH * T])
        h2r = sb("h2r", [128, KH * T])
        hab = sb("hab", [128, T])
        qmx = sb("qmx", [128, T])
        qcl = sb("qcl", [128, 16], F32)
        r1bc = sb("r1bc", [128, 2 * T])
        r2bc = sb("r2bc", [128, T])
        r3bc = sb("r3bc", [128, 2 * T])
        sbc = sb("sbc", [128, T], F32)
        wibc = sb("wibc", [128, T], F32)
        rowb = sb("rowb", [128, 2 * T], F32)
        rowc = sb("rowc", [128, 2 * T])
        xo = sb("xo", [128, T], F32)
        wiu = sb("wiu", [128, 2 * 4096])
        wxu = sb("wxu", [128, 2 * 2048])
        wbcu = sb("wbcu", [128, KD * 32])
        wou = sb("wou", [128, 2 * 2048])
        wfu = sb("wfu", [128, 2 * 2048])
        Ang = sb("Ang", [128, KD * DS], F32)
        Dcl = sb("Dcl", [128, KD], F32)
        cwc = sb("cwc", [128, KD * 4], F32)
        cbc = sb("cbc", [128, KD], F32)
        nwc = sb("nwc", [128, KH], F32)
        fbc = sb("fbc", [128, KH], F32)
        ca1 = sb("ca1", [128, KH], F32)
        sav = sb("sav", [128, KH], F32)
        gav = sb("gav", [128, KH], F32)
        cb1 = sb("cb1", [128, KH], F32)
        sbv = sb("sbv", [128, KH], F32)
        gbv = sb("gbv", [128, KH], F32)
        gbq = sb("gbq", [128, KH], F32)
        fb2 = sb("fb2", [128, KH], F32)
        wsc = sb("wsc", [128, 1], F32)
        crw = sb("crw", [128, KH], F32)
        slc = sb("slc", [128, KH], F32)
        ep5 = sb("ep5", [1, 1], F32)
        ep6 = sb("ep6", [1, 1], F32)
        idb = sb("idb", [128, 128])
        onb = sb("onb", [128, 1])
        onf = sb("onf", [128, 1], F32)

        psA = psum("psA", [128, T])
        psB = psum("psB", [128, T])
        pmm = [psum(f"pmm{j}", [128, T]) for j in range(4)]
        pbc = psum("pbc", [32, T])
        ptp = psum("ptp", [128, 256], BF16)

        def sem(name):
            return es.enter_context(nc.semaphore(name))

        SX = sem("SX"); SWI = sem("SWI"); SWX = sem("SWX"); SWO = sem("SWO")
        SWF = sem("SWF"); SB1 = sem("SB1"); SB2 = sem("SB2"); SBC = sem("SBC")
        SB3 = sem("SB3"); SBQ = sem("SBQ"); SOUT = sem("SOUT")
        SCST = sem("SCST"); SWLD = sem("SWLD"); SWST = sem("SWST")
        SWML = sem("SWML"); SWFS = sem("SWFS")
        t_st1 = sem("t_st1"); t_st2 = sem("t_st2"); t_st3 = sem("t_st3")
        t_win = sem("t_win"); t_wig = sem("t_wig"); t_bc = sem("t_bc")
        t_dlt = sem("t_dlt"); t_wxg = sem("t_wxg"); t_wo = sem("t_wo")
        t_wog = sem("t_wog"); t_ffn = sem("t_ffn"); t_qtp = sem("t_qtp")
        t_sqc = sem("t_sqc"); TSU1 = sem("TSU1"); TSU2 = sem("TSU2")
        v_var1 = sem("v_var1"); v_rw1 = sem("v_rw1"); v_hln = sem("v_hln")
        v_var2 = sem("v_var2"); v_h1 = sem("v_h1"); v_cnv = sem("v_cnv")
        v_bc16 = sem("v_bc16"); v_dlt = sem("v_dlt"); v_scan = sem("v_scan")
        v_v = sem("v_v"); v_xnw = sem("v_xnw"); v_var3 = sem("v_var3")
        v_rw3 = sem("v_rw3"); v_h2 = sem("v_h2"); v_qtr = sem("v_qtr")
        v_mxm = sem("v_mxm"); v_qrd = sem("v_qrd"); v_qrw = sem("v_qrw")
        v_r = sem("v_r"); v_out = sem("v_out")
        VSU1 = sem("VSU1"); VSU2 = sem("VSU2"); VSU3 = sem("VSU3")
        s_sq1 = sem("s_sq1"); s_rw1 = sem("s_rw1"); s_ev = sem("s_ev")
        s_cs = sem("s_cs"); s_zs = sem("s_zs"); s_sq2 = sem("s_sq2")
        s_rw2 = sem("s_rw2"); s_sp = sem("s_sp"); s_dA = sem("s_dA")
        s_rw3 = sem("s_rw3"); s_hab = sem("s_hab"); s_sq3 = sem("s_sq3")
        SSU1 = sem("SSU1"); SSU2 = sem("SSU2"); SSU3 = sem("SSU3")

        cast_slices = [("wi", k, 4096) for k in range(KH)] + \
                      [("wx", k, 2080) for k in range(KD)] + \
                      [("wo", k, 1024) for k in range(KD)]
        NSL = len(cast_slices)
        stores_per = [2 if kind == "wx" else 1 for kind, _, _ in cast_slices]
        stores_thru = np.cumsum([0] + stores_per)
        castA = xf[:, 0:4096]
        castB = xcv[:, :].bitcast(F32)
        cslot = [castA, castB]
        N_CST = 11

        def W(eng, s, v):
            if v > 0:
                eng.wait_ge(s, v)

        block = es.enter_context(nc.Block())

        @block.sync
        def _(sync):
            cst = [0]
            def dcst(dst, src):
                sync.dma_start(dst, src, allow_slow_non_contiguous=True
                               ).then_inc(SCST, 16)
                cst[0] += 1
            dcst(Ang[:, :].rearrange("p (g n) -> p g n", g=KD),
                 alog[:, :].rearrange("(g p) n -> p g n", p=128))
            dcst(Dcl[:, :], dprm[:].rearrange("(g p) -> p g", p=128))
            dcst(cwc[:, :].rearrange("p (g k) -> p g k", g=KD),
                 cnvw[:, :].rearrange("(g p) k -> p g k", p=128))
            dcst(cbc[:, :], cnvb[:].rearrange("(g p) -> p g", p=128))
            dcst(nwc[:, :], nrmw[:].rearrange("(g p) -> p g", p=128))
            dcst(fbc[:, :], fbia[:].rearrange("(g p) -> p g", p=128))
            dcst(idb[:, :], idbf[:, :])
            dcst(onb[:, :], onbf[:, :])
            dcst(onf[:, :], onf3[:, :])
            dcst(wsc[:, :], wsca[:, :].broadcast_to([128, 1]))
            dcst(crw[:, :], crow[:].rearrange("(g p) -> p g", p=128))
            dcst(scrA[:, 0:1024].bitcast(I32), pack[0:128, :])
            dcst(hln[:, :].bitcast(I32)[:, 0:1024], pack[128:256, :])
            assert cst[0] == N_CST + 2
            cast_list = cast_slices if su >= 2 else []
            for si, (kind, k, wid) in enumerate(cast_list):
                sl = si % 2
                if si >= 2:
                    W(sync, VSU1, si - 1)
                src = {"wi": w_in, "wx": w_xp, "wo": w_ou}[kind]
                sync.dma_start(cslot[sl][:, 0:wid],
                               src[k * 128:(k + 1) * 128, :]).then_inc(SWLD, 16)
                W(sync, VSU1, si + 1)
                cap = wiu[:, sl * 4096: sl * 4096 + wid]
                if kind == "wi":
                    sync.dma_start(
                        bass.AP(wi_s, k * 128 * 512,
                                [[512, 128], [KH * 128 * 512, 8], [1, 512]]),
                        cap.rearrange("p (g m) -> p g m", g=8),
                    ).then_inc(SWST, 16)
                elif kind == "wx":
                    sync.dma_start(
                        bass.AP(wxbc_s, k * 128 * 32, [[32, 128], [1, 32]]),
                        cap[:, 0:32],
                    ).then_inc(SWST, 16)
                    sync.dma_start(
                        bass.AP(wxd_s, k * 128 * 128,
                                [[128, 128], [KD * 128 * 128, 16], [1, 128]]),
                        cap[:, 32:2080].rearrange("p (g m) -> p g m", g=16),
                    ).then_inc(SWST, 16)
                else:
                    sync.dma_start(
                        bass.AP(wo_s, k * 128 * 128,
                                [[128, 128], [KD * 128 * 128, 8], [1, 128]]),
                        cap.rearrange("p (g m) -> p g m", g=8),
                    ).then_inc(SWST, 16)
            if su >= 2:
                W(sync, VSU1, NSL)
            for mc in range(48 if su >= 3 else 0):
                if mc >= 2:
                    W(sync, TSU1, mc - 1)
                sync.dma_start(
                    cslot[mc % 2][:, 0:1024].rearrange("p (g m) -> p g m", g=KH),
                    bass.AP(wmod, mc * 128,
                            [[6 * H, 128], [128 * 6 * H, KH], [1, 128]]),
                ).then_inc(SWML, 16)
            if su >= 4:
                W(sync, SSU2, 64)
            for u in range(4 if su >= 4 else 0):
                sync.dma_start(
                    bass.AP(wf_s, u * KH * 128 * 256,
                            [[256, 128], [128 * 256, KH], [1, 256]]),
                    Brep[:, :].rearrange("p (kc o) -> p kc o", kc=KH)
                    [:, :, u * 256:(u + 1) * 256],
                ).then_inc(SWFS, 16)

            if su >= 3:
                W(sync, TSU1, 48)
            if su >= 2:
                W(sync, SWST, 16 * int(stores_thru[NSL]))
            for i in range(NT):
                if ph < 1:
                    break
                if ph >= 5:
                    W(sync, v_xnw, 8 * i)
                else:
                    W(sync, v_hln, 8 * i)
                sync.dma_start(
                    xf[:, :].rearrange("p (g t) -> p g t", g=KH),
                    xin[:, i * T:(i + 1) * T].rearrange("(g p) t -> p g t",
                                                        p=128),
                ).then_inc(SX, 16)
                W(sync, v_rw1, i + 1)
                sync.dma_start(rows1[0:1, :], rowc[0:1, 0:T]).then_inc(SB1, 16)
                sync.dma_start(rows1[1:2, :], rowc[0:1, T:2 * T]).then_inc(SB1, 16)
                W(sync, SB1, 16 * (3 * i + 2))
                sync.dma_start(
                    r1bc[:, :].rearrange("p (a t) -> p a t", a=2),
                    rows1[:, :][None, :, :].broadcast_to([128, 2, T]),
                ).then_inc(SB1, 16)
                W(sync, s_rw2, i + 1)
                sync.dma_start(rows2[:, :], rowc[0:1, 0:T]).then_inc(SB2, 16)
                W(sync, SB2, 16 * (2 * i + 1))
                sync.dma_start(
                    r2bc[:, :], rows2[0, :][None, :].broadcast_to([128, T]),
                ).then_inc(SB2, 16)
                if ph < 2:
                    continue
                for u in range(8):
                    if u >= 2 or i > 0:
                        W(sync, t_win, 32 * i + 4 * u - 4)
                    sync.dma_start(
                        wiu[:, (u % 2) * 4096:(u % 2) * 4096 + 4096]
                        .rearrange("p (g m) -> p g m", g=8),
                        bass.AP(wi_s, u * KH * 128 * 512,
                                [[512, 128], [128 * 512, 8], [1, 512]]),
                    ).then_inc(SWI, 16)
                if ph < 3:
                    continue
                W(sync, t_bc, i)
                sync.dma_start(
                    wbcu[:, :].rearrange("p (g m) -> p g m", g=KD),
                    bass.AP(wxbc_s, 0, [[32, 128], [128 * 32, 16], [1, 32]]),
                ).then_inc(SWX, 16)
                W(sync, v_bc16, i + 1)
                sync.dma_start(bcscr[:, :], bc16[:, :]).then_inc(SBC, 16)
                W(sync, SBC, 16 * (3 * i + 1))
                if i == 0:
                    W(sync, SWFS, 64)
                if ph >= 4:
                    W(sync, v_v, 16 * i)
                sync.dma_start(
                    Brep[:, :].rearrange("p (n t) -> p n t", n=DS),
                    bcscr[0:DS, :][None, :, :].broadcast_to([128, DS, T]),
                ).then_inc(SBC, 16)
                sync.dma_start(
                    Crep[:, :].rearrange("p (n t) -> p n t", n=DS),
                    bcscr[DS:2 * DS, :][None, :, :].broadcast_to([128, DS, T]),
                ).then_inc(SBC, 16)
                for u in range(16):
                    W(sync, t_dlt, 16 * i + u - 1)
                    sync.dma_start(
                        wxu[:, (u % 2) * 2048:(u % 2) * 2048 + 2048]
                        .rearrange("p (g m) -> p g m", g=KD),
                        bass.AP(wxd_s, u * KD * 128 * 128,
                                [[128, 128], [128 * 128, 16], [1, 128]]),
                    ).then_inc(SWX, 16)
                if ph < 5:
                    continue
                for u in range(8):
                    W(sync, t_wo, 8 * i + u - 1)
                    sync.dma_start(
                        wou[:, (u % 2) * 2048:(u % 2) * 2048 + 2048]
                        .rearrange("p (g m) -> p g m", g=KD),
                        bass.AP(wo_s, u * KD * 128 * 128,
                                [[128, 128], [128 * 128, 16], [1, 128]]),
                    ).then_inc(SWO, 16)
                if ph < 6:
                    continue
                W(sync, v_rw3, i + 1)
                sync.dma_start(rows3[0:1, :], rowc[0:1, 0:T]).then_inc(SB3, 16)
                sync.dma_start(rows3[1:2, :], rowc[0:1, T:2 * T]).then_inc(SB3, 16)
                W(sync, SB3, 16 * (3 * i + 2))
                sync.dma_start(
                    r3bc[:, :].rearrange("p (a t) -> p a t", a=2),
                    rows3[:, :][None, :, :].broadcast_to([128, 2, T]),
                ).then_inc(SB3, 16)
                W(sync, v_qrw, i + 1)
                sync.dma_start(bass.AP(qsscr, 0, [[1, 128], [128, 4]]),
                               qcl[:, 0:4], allow_slow_non_contiguous=True
                               ).then_inc(SBQ, 16)
                W(sync, SBQ, 16 * (2 * i + 1))
                sync.dma_start(
                    wibc[:, :], qsscr[:][None, :].broadcast_to([128, T]),
                ).then_inc(SBQ, 16)
                if ph < 7:
                    continue
                for mc in range(KH):
                    if mc % 2 == 0:
                        u = mc // 2
                        W(sync, t_ffn, 8 * i + 2 * u - 2)
                        sync.dma_start(
                            wfu[:, (u % 2) * 2048:(u % 2) * 2048 + 2048]
                            .rearrange("p (g m) -> p g m", g=KH),
                            bass.AP(wf_s, u * KH * 128 * 256,
                                    [[256, 128], [128 * 256, 8], [1, 256]]),
                        ).then_inc(SWF, 16)
                    W(sync, v_out, 8 * i + mc + 1)
                    sync.dma_start(
                        out[mc * 128:(mc + 1) * 128, i * T:(i + 1) * T],
                        xo[:, :],
                    ).then_inc(SOUT, 16)
            if ph >= 7:
                sync.wait_ge(SOUT, 16 * 8 * NT)
                if int(os.environ.get("KDBG", "0")):
                    W(sync, v_var1, NT + 1)
                    nd = [0]
                    def dmp(j, ap):
                        sync.dma_start(dbg[j, :, 0:ap.shape[1]], ap
                                       ).then_inc(SCST, 16)
                        nd[0] += 1
                    dmp(0, stt[:, :])
                    dmp(1, scrB[:, :])
                    dmp(2, yac[:, :])
                    dmp(3, vsc[:, :])
                    for gi, gap in enumerate((sav, ca1, gav, sbv, cb1,
                                              gbv, fb2)):
                        sync.dma_start(dbg[4, :, gi * 8:gi * 8 + 8],
                                       gap[:, :]).then_inc(SCST, 16)
                        nd[0] += 1
                    def dmh(j, ap):
                        sync.dma_start(dbh[j, :, 0:ap.shape[1]], ap
                                       ).then_inc(SCST, 16)
                        nd[0] += 1
                    dmh(0, hln[:, :])
                    dmh(1, h1[:, :])
                    dmh(2, xcv[:, :])
                    dmh(3, sz[:, :])
                    dmh(4, xnw[:, :])
                    dmh(5, h2r[:, :])
                    sync.wait_ge(SCST, 16 * (N_CST + 2 + nd[0]))
            elif ph == 6:
                sync.wait_ge(v_r, 8 * NT)
            elif ph == 5:
                sync.wait_ge(v_xnw, 8 * NT)
            elif ph == 4:
                sync.wait_ge(v_v, 16 * NT)
            elif ph == 3:
                sync.wait_ge(t_dlt, 16 * NT)
            elif ph == 2:
                sync.wait_ge(s_zs, 16 * NT)
            elif ph == 1:
                sync.wait_ge(v_h1, 8 * NT)
            else:
                if su >= 3:
                    sync.wait_ge(VSU3, 1)
                if su >= 4:
                    sync.wait_ge(SSU2, 64)
                    sync.wait_ge(SWFS, 64)
                if su == 2:
                    sync.wait_ge(VSU1, NSL)
                sync.wait_ge(SSU3, 1)

        @block.tensor
        def _(tensor):
            if su >= 3:
                W(tensor, SSU1, 1)
            for mc in range(48 if su >= 3 else 0):
                W(tensor, SWML, 16 * (mc + 1))
                for k in range(KH):
                    mm = tensor.matmul(
                        psA[:, mc:mc + 1],
                        cslot[mc % 2][:, k * 128:(k + 1) * 128],
                        slc[:, k:k + 1],
                        start=(k == 0), stop=(k == KH - 1))
                mm.then_inc(TSU1, 1)
            if su >= 4:
                W(tensor, VSU2, 1)
            for j in range(64 if su >= 4 else 0):
                W(tensor, SSU2, j)
                tensor.transpose(
                    ptp[:, (j % 2) * 128:(j % 2) * 128 + 128],
                    sz[:, j * 128:(j + 1) * 128],
                    idb[:, :],
                ).then_inc(TSU2, 1)

            W(tensor, SCST, 16 * (N_CST + 2))
            W(tensor, VSU3, 1)
            for i in range(NT):
                if ph < 1:
                    break
                W(tensor, SX, 16 * (i + 1))
                W(tensor, v_var3, i)
                for kc in range(KH):
                    tensor.matmul(psA[0:1, :], onf[:, :],
                                  xf[:, kc * T:(kc + 1) * T],
                                  start=(kc == 0), stop=(kc == KH - 1))
                for kc in range(KH):
                    W(tensor, s_sq1, 8 * i + kc + 1)
                    mm = tensor.matmul(psB[0:1, :], onb[:, :],
                                       bsq[:, (kc % 2) * T:(kc % 2) * T + T],
                                       start=(kc == 0), stop=(kc == KH - 1))
                    mm.then_inc(t_sqc, 1)
                W(tensor, v_var1, i + 1)
                for kc in range(KH):
                    W(tensor, s_sq2, 8 * i + kc + 1)
                    mm = tensor.matmul(psA[0:1, :], onb[:, :],
                                       bsq[:, (kc % 2) * T:(kc % 2) * T + T],
                                       start=(kc == 0), stop=(kc == KH - 1))
                    mm.then_inc(t_sqc, 1)
                if ph < 2:
                    continue
                W(tensor, v_h1, 8 * (i + 1))
                for g in range(4):
                    W(tensor, SWI, 16 * (8 * i + g + 1))
                    for ml in range(4):
                        mc = 4 * g + ml
                        W(tensor, s_ev, 16 * i + mc - 3)
                        for kc in range(KH):
                            mm = tensor.matmul(
                                pmm[ml][:, :],
                                wiu[:, (g % 2) * 4096 + kc * 512 + ml * 128:
                                    (g % 2) * 4096 + kc * 512 + (ml + 1) * 128],
                                h1[:, kc * T:(kc + 1) * T],
                                start=(kc == 0), stop=(kc == KH - 1))
                        mm.then_inc(t_win, 1)
                for g in range(4):
                    W(tensor, SWI, 16 * (8 * i + 4 + g + 1))
                    if g == 0:
                        W(tensor, s_ev, 16 * (i + 1))
                    for ml in range(4):
                        mc = 4 * g + ml
                        if g > 0:
                            W(tensor, s_zs, 16 * i + mc - 3)
                        for kc in range(KH):
                            mm = tensor.matmul(
                                pmm[ml][:, :],
                                wiu[:, ((4 + g) % 2) * 4096 + kc * 512 + ml * 128:
                                    ((4 + g) % 2) * 4096 + kc * 512 + (ml + 1) * 128],
                                h1[:, kc * T:(kc + 1) * T],
                                start=(kc == 0), stop=(kc == KH - 1))
                        mm.then_inc(t_win, 1)
                if ph < 3:
                    continue
                W(tensor, SWX, 16 * (17 * i + 1))
                W(tensor, s_cs, 16 * (i + 1))
                W(tensor, s_zs, 16 * (i + 1))
                W(tensor, v_bc16, i)
                for kc in range(KD):
                    mm = tensor.matmul(pbc[:, :],
                                       wbcu[:, kc * 32:(kc + 1) * 32],
                                       xcv[:, kc * T:(kc + 1) * T],
                                       start=(kc == 0), stop=(kc == KD - 1))
                mm.then_inc(t_bc, 1)
                for dg in range(KD):
                    W(tensor, SWX, 16 * (17 * i + dg + 2))
                    if dg >= 2 and ph >= 4:
                        W(tensor, s_sp, 16 * i + dg - 1)
                    for kc in range(KD):
                        mm = tensor.matmul(
                            pmm[dg % 2][:, :],
                            wxu[:, (dg % 2) * 2048 + kc * 128:
                                (dg % 2) * 2048 + (kc + 1) * 128],
                            xcv[:, kc * T:(kc + 1) * T],
                            start=(kc == 0), stop=(kc == KD - 1))
                    mm.then_inc(t_dlt, 1)
                if ph < 5:
                    continue
                W(tensor, v_v, 16 * (i + 1))
                for mc in range(KH):
                    W(tensor, SWO, 16 * (8 * i + mc + 1))
                    W(tensor, v_xnw, 8 * i + mc - 1)
                    for kc in range(KD):
                        mm = tensor.matmul(
                            pmm[2 + mc % 2][:, :],
                            wou[:, (mc % 2) * 2048 + kc * 128:
                                (mc % 2) * 2048 + (kc + 1) * 128],
                            sz[:, kc * T:(kc + 1) * T],
                            start=(kc == 0), stop=(kc == KD - 1))
                    mm.then_inc(t_wo, 1)
                if ph < 6:
                    continue
                W(tensor, v_xnw, 8 * (i + 1))
                W(tensor, v_var2, i + 1)
                for kc in range(KH):
                    tensor.matmul(psA[0:1, :], onb[:, :],
                                  xnw[:, kc * T:(kc + 1) * T],
                                  start=(kc == 0), stop=(kc == KH - 1))
                for kc in range(KH):
                    W(tensor, s_sq3, 8 * i + kc + 1)
                    mm = tensor.matmul(psB[0:1, :], onb[:, :],
                                       bsq[:, (kc % 2) * T:(kc % 2) * T + T],
                                       start=(kc == 0), stop=(kc == KH - 1))
                    mm.then_inc(t_sqc, 1)
                W(tensor, v_qtr, 8 * (i + 1))
                for j in range(4):
                    W(tensor, v_qrd, 4 * i + j)
                    tensor.transpose(ptp[:, (j % 2) * 128:(j % 2) * 128 + 128],
                                     qmx[:, j * 128:(j + 1) * 128],
                                     idb[:, :]).then_inc(t_qtp, 1)
                if ph < 7:
                    continue
                W(tensor, v_r, 8 * (i + 1))
                for mc in range(KH):
                    u = mc // 2
                    W(tensor, SWF, 16 * (4 * i + u + 1))
                    W(tensor, v_out, 8 * i + mc - 1)
                    for kc in range(KH):
                        mm = tensor.matmul(
                            pmm[mc % 2][:, :],
                            wfu[:, (u % 2) * 2048 + kc * 256 + (mc % 2) * 128:
                                (u % 2) * 2048 + kc * 256 + (mc % 2) * 128 + 128],
                            h2r[:, kc * T:(kc + 1) * T],
                            start=(kc == 0), stop=(kc == KH - 1))
                    mm.then_inc(t_ffn, 1)

        @block.vector
        def _(vector):
            vector.memset(stt[:, :], 0.0)
            vector.memset(halo[:, :], 0.0)
            vector.memset(ep5[:, :], 1e-5)
            vector.memset(ep6[:, :], 1e-6)
            for j in range(2):
                for q in range(NQ):
                    vector.memset(dAb[j][:, q * BL:q * BL + 1], 0.0)
            for si, (kind, k, wid) in enumerate(cast_slices if su >= 2 else []):
                sl = si % 2
                W(vector, SWLD, 16 * (si + 1))
                if si >= 2:
                    W(vector, SWST, 16 * int(stores_thru[si - 1]))
                vector.tensor_copy(wiu[:, sl * 4096: sl * 4096 + wid],
                                   cslot[sl][:, 0:wid]).then_inc(VSU1, 1)
            if su < 4:
                vector.engine_nop().then_inc(VSU2, 1)
            else:
                W(vector, SCST, 16 * (N_CST + 2))
                ti = bsq[:, :].bitcast(I32)
                for s4 in range(4):
                    for ot in range(2):
                        srcw = (scrA[:, 0:1024].bitcast(I32) if ot == 0
                                else hln[:, :].bitcast(I32)[:, 0:1024])
                        for hf in range(2):
                            sw = srcw[:, hf * 512:(hf + 1) * 512]
                            if s4 == 0:
                                vector.tensor_scalar(ti[:, 0:512], sw, 3, None,
                                                     ALU.bitwise_and)
                            else:
                                vector.tensor_scalar(ti[:, 0:512], sw, 2 * s4,
                                                     None, ALU.logical_shift_right)
                                vector.tensor_scalar(ti[:, 0:512], ti[:, 0:512],
                                                     3, None, ALU.bitwise_and)
                            vector.tensor_scalar(ti[:, 0:512], ti[:, 0:512], 1,
                                                 None, ALU.subtract)
                            vector.tensor_copy(
                                sz[:, (s4 * 2 + ot) * 1024 + hf * 512:
                                   (s4 * 2 + ot) * 1024 + (hf + 1) * 512],
                                ti[:, 0:512])
                vector.engine_nop().then_inc(VSU2, 1)
            if su < 3:
                vector.engine_nop().then_inc(VSU3, 1)
            else:
                W(vector, TSU1, 48)
                vector.tensor_copy(sav[:, :], psA[:, 0:8])
                vector.tensor_scalar(ca1[:, :], psA[:, 8:16], 1.0, None, ALU.add)
                vector.tensor_copy(sbv[:, :], psA[:, 24:32])
                vector.tensor_scalar(cb1[:, :], psA[:, 32:40], 1.0, None, ALU.add)
                g = scrA[:, 0 * 64:0 * 64 + 48]
                a = scrA[:, 1 * 64:1 * 64 + 48]
                b = scrA[:, 2 * 64:2 * 64 + 48]
                m = scrA[:, 3 * 64:3 * 64 + 48]
                d = scrA[:, 4 * 64:4 * 64 + 48]
                e = scrA[:, 5 * 64:5 * 64 + 48]
                r = scrA[:, 6 * 64:6 * 64 + 48]
                vector.tensor_copy(g, psA[:, 0:48])
                vector.tensor_scalar(a, g, 0.25, None, ALU.max)
                vector.tensor_scalar(b, g, -0.25, None, ALU.min)
                vector.tensor_scalar(m, g, 0.0, None, ALU.is_ge)
                vector.tensor_tensor(d, a, b, ALU.subtract)
                vector.tensor_tensor(e, m, d, ALU.mult)
                vector.tensor_tensor(r, b, e, ALU.add)
                vector.tensor_copy(gav[:, :], r[:, 16:24])
                vector.tensor_copy(gbv[:, :], r[:, 40:48])
                vector.tensor_scalar(gbq[:, :], r[:, 40:48], wsc[:, 0:1],
                                     1.0 / 127.0, ALU.mult, ALU.mult)
                vector.tensor_tensor(fb2[:, :], fbc[:, :], r[:, 40:48],
                                     ALU.mult).then_inc(VSU3, 1)

            for i in range(NT):
                if ph < 1:
                    break
                W(vector, t_sqc, 24 * i + 8)
                vector.tensor_scalar(rowb[0:1, 0:T], psA[0:1, :], 1.0 / H,
                                     None, ALU.mult)
                vector.tensor_tensor(rowb[0:1, T:2 * T], rowb[0:1, 0:T],
                                     rowb[0:1, 0:T], ALU.mult)
                vector.scalar_tensor_tensor(rowb[0:1, T:2 * T], psB[0:1, :],
                                            1.0 / H, rowb[0:1, T:2 * T],
                                            ALU.mult, ALU.subtract
                                            ).then_inc(v_var1, 1)
                W(vector, s_rw1, i + 1)
                W(vector, SB3, 16 * (3 * i - 1))
                vector.scalar_tensor_tensor(rowc[0:1, T:2 * T],
                                            rowb[0:1, 0:T], -1.0,
                                            rowc[0:1, 0:T],
                                            ALU.mult, ALU.mult
                                            ).then_inc(v_rw1, 1)
                W(vector, SB1, 16 * (3 * i + 3))
                for kc in range(KH):
                    sA = scrA[:, (kc % 2) * T:(kc % 2 + 1) * T]
                    vector.tensor_tensor(sA, xf[:, kc * T:(kc + 1) * T],
                                         r1bc[:, 0:T], ALU.mult)
                    vector.tensor_tensor(sA, sA, r1bc[:, T:2 * T], ALU.add)
                    vector.tensor_scalar(hln[:, kc * T:(kc + 1) * T], sA,
                                         ca1[:, kc:kc + 1], sav[:, kc:kc + 1],
                                         ALU.mult, ALU.add).then_inc(v_hln, 1)
                W(vector, t_sqc, 24 * i + 16)
                vector.tensor_scalar(rowb[0:1, T:2 * T], psA[0:1, :], 1.0 / H,
                                     None, ALU.mult).then_inc(v_var2, 1)
                W(vector, SB2, 16 * (2 * i + 2))
                if ph >= 2:
                    W(vector, t_win, 32 * i)
                for kc in range(KH):
                    vector.scalar_tensor_tensor(
                        h1[:, kc * T:(kc + 1) * T],
                        hln[:, kc * T:(kc + 1) * T],
                        nwc[:, kc:kc + 1], r2bc[:, :],
                        ALU.mult, ALU.mult).then_inc(v_h1, 1)
                if ph < 2:
                    continue
                for dg in range(KD):
                    W(vector, s_ev, 16 * i + dg + 1)
                    W(vector, s_cs, 16 * i + dg)
                    vector.tensor_copy(xss[:, 0:3], halo[:, dg * 3:dg * 3 + 3])
                    vector.tensor_scalar(cacc[:, :], xss[:, 0:T],
                                         cwc[:, dg * 4:dg * 4 + 1], None,
                                         ALU.mult)
                    for kk in range(1, 4):
                        vector.scalar_tensor_tensor(
                            cacc[:, :], xss[:, kk:kk + T],
                            cwc[:, dg * 4 + kk:dg * 4 + kk + 1], cacc[:, :],
                            ALU.mult, ALU.add)
                    vector.tensor_copy(halo[:, dg * 3:dg * 3 + 3],
                                       xss[:, T:T + 3]).then_inc(v_cnv, 1)
                if ph < 3:
                    continue
                W(vector, t_bc, i + 1)
                vector.tensor_copy(bc16[:, :], pbc[:, :]).then_inc(v_bc16, 1)
                if ph < 4:
                    continue
                for dg in range(KD):
                    W(vector, s_sp, 16 * i + dg + 1)
                    if dg >= 2:
                        W(vector, s_dA, 8 * KD * i + 8 * dg - 8)
                    dsl = dlt[:, (dg % 2) * T:(dg % 2 + 1) * T]
                    vector.tensor_tensor(dsl, scrA[:, (dg % 2) * T:
                                                    (dg % 2 + 1) * T],
                                         scrB[:, :], ALU.add)
                    vector.tensor_tensor(dxs[:, :], dsl,
                                         xcv[:, dg * T:(dg + 1) * T], ALU.mult
                                         ).then_inc(v_dlt, 1)
                    if dg == 0:
                        W(vector, SBC, 16 * (3 * i + 3))
                    for q in range(NSC):
                        st0 = dg * DS + NQ * q
                        ubv = ub[:, :].rearrange("p (q b) -> p q b", q=NQ)
                        hbv = hb[:, :].rearrange("p (q b) -> p q b", q=NQ)
                        for blk in range(NQ):
                            vector.tensor_copy(
                                ub[:, blk * BL:blk * BL + 1],
                                stt[:, st0 + blk:st0 + blk + 1])
                        vector.tensor_tensor(
                            ubv[:, :, 1:BL],
                            dxs[:, None, :].broadcast_to([128, NQ, T]),
                            Brep[:, :].rearrange("p (n t) -> p n t", n=DS)
                            [:, NQ * q:NQ * q + NQ, :],
                            ALU.mult)
                        W(vector, s_dA, 8 * KD * i + 8 * dg + q + 1)
                        vector.tensor_tensor_scan(
                            hb[:, :], dAb[(NSC * dg + q) % 2][:, :], ub[:, :],
                            0.0, ALU.mult, ALU.add).then_inc(v_scan, 1)
                        for blk in range(NQ):
                            vector.tensor_copy(
                                stt[:, st0 + blk:st0 + blk + 1],
                                hb[:, blk * BL + BL - 1:blk * BL + BL])
                        vector.tensor_tensor(
                            ubv[:, :, 1:BL], hbv[:, :, 1:BL],
                            Crep[:, :].rearrange("p (n t) -> p n t", n=DS)
                            [:, NQ * q:NQ * q + NQ, :],
                            ALU.mult)
                        if q == 0:
                            vector.tensor_tensor(yac[:, :], ubv[:, 0, 1:BL],
                                                 ubv[:, 1, 1:BL], ALU.add)
                        else:
                            vector.tensor_tensor(yac[:, :], yac[:, :],
                                                 ubv[:, 0, 1:BL], ALU.add)
                            vector.tensor_tensor(yac[:, :], yac[:, :],
                                                 ubv[:, 1, 1:BL], ALU.add)
                    W(vector, s_zs, 16 * i + dg + 1)
                    vector.scalar_tensor_tensor(
                        vsc[:, :], xcv[:, dg * T:(dg + 1) * T],
                        Dcl[:, dg:dg + 1], yac[:, :], ALU.mult, ALU.add)
                    vector.tensor_tensor(sz[:, dg * T:(dg + 1) * T],
                                         vsc[:, :], sz[:, dg * T:(dg + 1) * T],
                                         ALU.mult).then_inc(v_v, 1)
                if ph < 5:
                    continue
                for mc in range(KH):
                    W(vector, t_wo, 8 * i + mc + 1)
                    esl = scrA[:, (mc % 2) * T:(mc % 2 + 1) * T]
                    vector.tensor_tensor(esl, pmm[2 + mc % 2][:, :],
                                         hln[:, mc * T:(mc + 1) * T], ALU.add)
                    vector.scalar_tensor_tensor(
                        xnw[:, mc * T:(mc + 1) * T], esl,
                        gav[:, mc:mc + 1], xf[:, mc * T:(mc + 1) * T],
                        ALU.mult, ALU.add).then_inc(v_xnw, 1)
                if ph < 6:
                    continue
                W(vector, t_sqc, 24 * i + 24)
                vector.tensor_scalar(rowb[0:1, 0:T], psA[0:1, :], 1.0 / H,
                                     None, ALU.mult)
                vector.tensor_tensor(rowb[0:1, T:2 * T], rowb[0:1, 0:T],
                                     rowb[0:1, 0:T], ALU.mult)
                vector.scalar_tensor_tensor(rowb[0:1, T:2 * T], psB[0:1, :],
                                            1.0 / H, rowb[0:1, T:2 * T],
                                            ALU.mult, ALU.subtract
                                            ).then_inc(v_var3, 1)
                W(vector, s_rw3, i + 1)
                vector.scalar_tensor_tensor(rowc[0:1, T:2 * T],
                                            rowb[0:1, 0:T], -1.0,
                                            rowc[0:1, 0:T],
                                            ALU.mult, ALU.mult
                                            ).then_inc(v_rw3, 1)
                W(vector, SB3, 16 * (3 * i + 3))
                for kc in range(KH):
                    sA = scrA[:, (kc % 2) * T:(kc % 2 + 1) * T]
                    vector.tensor_tensor(sA, xnw[:, kc * T:(kc + 1) * T],
                                         r3bc[:, 0:T], ALU.mult)
                    vector.tensor_tensor(sA, sA, r3bc[:, T:2 * T], ALU.add)
                    vector.tensor_scalar(h2r[:, kc * T:(kc + 1) * T], sA,
                                         cb1[:, kc:kc + 1], sbv[:, kc:kc + 1],
                                         ALU.mult, ALU.add).then_inc(v_h2, 1)
                W(vector, t_qtp, 4 * i)
                for kc in range(KH):
                    W(vector, s_hab, 8 * i + kc + 1)
                    if kc == 0:
                        vector.tensor_copy(qmx[:, :], hab[:, :]
                                           ).then_inc(v_qtr, 1)
                    else:
                        vector.tensor_tensor(qmx[:, :], qmx[:, :],
                                             hab[:, :], ALU.max
                                             ).then_inc(v_qtr, 1)
                for j in range(4):
                    W(vector, t_qtp, 4 * i + j + 1)
                    vector.reduce_max(qcl[:, j:j + 1],
                                      ptp[:, (j % 2) * 128:(j % 2) * 128 + 128],
                                      AX.X).then_inc(v_qrd, 1)
                vector.engine_nop().then_inc(v_qrw, 1)
                W(vector, SBQ, 16 * (2 * i + 2))
                vector.tensor_scalar(wibc[:, :], wibc[:, :], 1e-5, None,
                                     ALU.max)
                vector.reciprocal(sbc[:, :], wibc[:, :])
                vector.tensor_scalar(sbc[:, :], sbc[:, :], 127.0, None,
                                     ALU.mult)
                for kc in range(KH):
                    sA = scrA[:, (kc % 2) * T:(kc % 2 + 1) * T]
                    vector.tensor_tensor(sA, h2r[:, kc * T:(kc + 1) * T],
                                         sbc[:, :], ALU.mult)
                    vector.tensor_scalar(h2r[:, kc * T:(kc + 1) * T], sA,
                                         RC, RC, ALU.add, ALU.subtract
                                         ).then_inc(v_r, 1)
                if ph < 7:
                    continue
                for mc in range(KH):
                    W(vector, t_ffn, 8 * i + mc + 1)
                    sA = scrA[:, (mc % 2) * T:(mc % 2 + 1) * T]
                    vector.tensor_tensor(sA, pmm[mc % 2][:, :], wibc[:, :],
                                         ALU.mult)
                    vector.tensor_scalar(scrB[:, :], sA,
                                         gbq[:, mc:mc + 1], fb2[:, mc:mc + 1],
                                         ALU.mult, ALU.add)
                    W(vector, SOUT, 16 * (8 * i + mc))
                    vector.tensor_tensor(xo[:, :], scrB[:, :],
                                         xnw[:, mc * T:(mc + 1) * T], ALU.add
                                         ).then_inc(v_out, 1)

            if int(os.environ.get("KDBG", "0")) and ph >= 7:
                W(vector, SOUT, 16 * 8 * NT)
                vector.tensor_copy(scrB[:, :], pmm[2][:, :])
                vector.tensor_copy(vsc[:, :], pmm[3][:, :]).then_inc(v_var1, 1)

        @block.scalar
        def _(scalar):
            W(scalar, SCST, 16 * (N_CST + 2))
            scalar.activation(slc[:, :], crw[:, :], AF.Silu
                              ).then_inc(SSU1, 1)
            scalar.activation(Ang[:, :], Ang[:, :], AF.Exp)
            scalar.activation(Ang[:, :], Ang[:, :], AF.Copy, bias=0.0,
                              scale=-1.0).then_inc(SSU3, 1)
            for j in range(64 if su >= 4 else 0):
                W(scalar, TSU2, j + 1)
                s4, rem = divmod(j, 2 * KH)
                ot, ib = divmod(rem, KH)
                o0 = s4 * 256 + ot * 128
                scalar.copy(Brep[:, ib * H + o0: ib * H + o0 + 128],
                            ptp[:, (j % 2) * 128:(j % 2) * 128 + 128]
                            ).then_inc(SSU2, 1)

            for i in range(NT):
                if ph < 1:
                    break
                W(scalar, SX, 16 * (i + 1))
                for kc in range(KH):
                    W(scalar, t_sqc, 24 * i + kc - 1)
                    scalar.activation(bsq[:, (kc % 2) * T:(kc % 2 + 1) * T],
                                      xf[:, kc * T:(kc + 1) * T], AF.Square
                                      ).then_inc(s_sq1, 1)
                W(scalar, v_var1, i + 1)
                W(scalar, SB3, 16 * (3 * i - 1))
                scalar.activation(rowb[0:1, T:2 * T], rowb[0:1, T:2 * T],
                                  AF.Ln, bias=ep5[:, :], scale=1.0)
                scalar.activation(rowc[0:1, 0:T], rowb[0:1, T:2 * T], AF.Exp,
                                  bias=0.0, scale=-0.5).then_inc(s_rw1, 1)
                for kc in range(KH):
                    W(scalar, v_hln, 8 * i + kc + 1)
                    W(scalar, t_sqc, 24 * i + 8 + kc - 1)
                    scalar.activation(bsq[:, (kc % 2) * T:(kc % 2 + 1) * T],
                                      hln[:, kc * T:(kc + 1) * T], AF.Square
                                      ).then_inc(s_sq2, 1)
                W(scalar, v_var2, i + 1)
                W(scalar, SB1, 16 * (3 * i + 2))
                scalar.activation(rowb[0:1, T:2 * T], rowb[0:1, T:2 * T],
                                  AF.Ln, bias=ep6[:, :], scale=1.0)
                scalar.activation(rowc[0:1, 0:T], rowb[0:1, T:2 * T], AF.Exp,
                                  bias=0.0, scale=-0.5).then_inc(s_rw2, 1)
                if ph < 2:
                    continue
                if ph >= 4:
                    W(scalar, v_v, 16 * i)
                for dg in range(KD):
                    W(scalar, t_win, 32 * i + dg + 1)
                    W(scalar, v_cnv, 16 * i + dg)
                    scalar.copy(xss[:, 3:T + 3], pmm[dg % 4][:, :]
                                ).then_inc(s_ev, 1)
                    W(scalar, v_cnv, 16 * i + dg + 1)
                    scalar.activation(xcv[:, dg * T:(dg + 1) * T],
                                      cacc[:, :], AF.Silu,
                                      bias=cbc[:, dg:dg + 1],
                                      scale=1.0).then_inc(s_cs, 1)
                if ph >= 5:
                    W(scalar, t_wo, 8 * i)
                for dg in range(KD):
                    W(scalar, t_win, 32 * i + 16 + dg + 1)
                    scalar.activation(sz[:, dg * T:(dg + 1) * T],
                                      pmm[dg % 4][:, :], AF.Silu
                                      ).then_inc(s_zs, 1)
                if ph < 4:
                    continue
                for dg in range(KD):
                    W(scalar, t_dlt, 16 * i + dg + 1)
                    if dg >= 1:
                        W(scalar, v_dlt, 16 * i + dg)
                    asl = scrA[:, (dg % 2) * T:(dg % 2 + 1) * T]
                    scalar.activation(asl, pmm[dg % 2][:, :], AF.Abs)
                    scalar.activation(asl, asl, AF.Exp, bias=0.0, scale=-1.0)
                    scalar.activation(asl, asl, AF.Ln, bias=onf[:, :],
                                      scale=1.0)
                    scalar.activation(scrB[:, :], pmm[dg % 2][:, :], AF.Relu
                                      ).then_inc(s_sp, 1)
                    W(scalar, v_dlt, 16 * i + dg + 1)
                    dsl = dlt[:, (dg % 2) * T:(dg % 2 + 1) * T]
                    for q in range(NSC):
                        W(scalar, v_scan, 8 * KD * i + 8 * dg + q - 1)
                        buf = dAb[(NSC * dg + q) % 2]
                        for jj in range(NQ):
                            n = NQ * q + jj
                            aa = scalar.activation(
                                buf[:, jj * BL + 1: jj * BL + 1 + T],
                                dsl, AF.Exp, bias=0.0,
                                scale=Ang[:, dg * DS + n:dg * DS + n + 1])
                        aa.then_inc(s_dA, 1)
                for kc in range(KH):
                    W(scalar, v_xnw, 8 * i + kc + 1)
                    W(scalar, t_sqc, 24 * i + 16 + kc - 1)
                    scalar.activation(bsq[:, (kc % 2) * T:(kc % 2 + 1) * T],
                                      xnw[:, kc * T:(kc + 1) * T], AF.Square
                                      ).then_inc(s_sq3, 1)
                W(scalar, v_var3, i + 1)
                W(scalar, SB2, 16 * (2 * i + 1))
                scalar.activation(rowb[0:1, T:2 * T], rowb[0:1, T:2 * T],
                                  AF.Ln, bias=ep5[:, :], scale=1.0)
                scalar.activation(rowc[0:1, 0:T], rowb[0:1, T:2 * T], AF.Exp,
                                  bias=0.0, scale=-0.5).then_inc(s_rw3, 1)
                for kc in range(KH):
                    W(scalar, v_h2, 8 * i + kc + 1)
                    W(scalar, v_qtr, 8 * i + kc)
                    scalar.activation(hab[:, :],
                                      h2r[:, kc * T:(kc + 1) * T], AF.Abs
                                      ).then_inc(s_hab, 1)

    return nc


_NC_CACHE = {}


def kernel(**inputs):
    x = np.asarray(inputs["L0_attn_x"])
    B = x.shape[0]
    if "nc" not in _NC_CACHE:
        _NC_CACHE["nc"] = build_nc()
    nc = _NC_CACHE["nc"]

    common = {
        "w_in": np.ascontiguousarray(inputs["L0_attn_w_in"], np.float32),
        "w_xp": np.ascontiguousarray(inputs["L0_attn_w_xproj"], np.float32),
        "w_ou": np.ascontiguousarray(inputs["L0_attn_w_out"], np.float32),
        "wmod": np.ascontiguousarray(inputs["L0_wmod"], np.float32),
        "alog": np.ascontiguousarray(inputs["L0_attn_A_log"], np.float32),
        "dprm": np.ascontiguousarray(inputs["L0_attn_D_param"], np.float32),
        "cnvw": np.ascontiguousarray(
            np.asarray(inputs["L0_attn_conv_w"]).reshape(DI, 4), np.float32),
        "cnvb": np.ascontiguousarray(inputs["L0_attn_conv_b"], np.float32),
        "nrmw": np.ascontiguousarray(inputs["L0_attn_norm_w"], np.float32),
        "fbia": np.ascontiguousarray(inputs["L0_ffn_bias"], np.float32),
        "wsca": np.ascontiguousarray(
            np.asarray(inputs["L0_ffn_weight_scale"]).reshape(1, 1),
            np.float32),
        "pack": np.ascontiguousarray(
            np.asarray(inputs["L0_ffn_packed_weight"]).reshape(256, 1024),
            np.int32),
        "idbf": np.eye(128, dtype=ml_dtypes.bfloat16),
        "onbf": np.ones((128, 1), dtype=ml_dtypes.bfloat16),
        "onf3": np.ones((128, 1), dtype=np.float32),
    }
    cmat = np.asarray(inputs["L0_c"], np.float32)
    in_maps = []
    for b in range(B):
        m = dict(common)
        m["xin"] = np.ascontiguousarray(x[b].T, np.float32)
        m["crow"] = np.ascontiguousarray(cmat[b], np.float32)
        in_maps.append(m)

    res = run_bass_kernel_spmd(nc, in_maps, core_ids=list(range(NCORE)))
    outs = [np.asarray(r["out"]).T for r in res.results]
    return np.stack(outs, axis=0)


if __name__ == "__main__":
    build_nc()
    print("build ok")


# version 06
# speedup vs baseline: 1.0476x; 1.0476x over previous
import contextlib
import os
import numpy as np
import ml_dtypes
import concourse.bass as bass
import concourse.mybir as mybir
from concourse.bass_utils import run_bass_kernel_spmd

F32 = mybir.dt.float32
BF16 = mybir.dt.bfloat16
I32 = mybir.dt.int32
AF = mybir.ActivationFunctionType
ALU = mybir.AluOpType
AX = mybir.AxisListType

H = 1024
L = 8192
DI = 2048
DS = 16
T = 512
NT = L // T
KH = H // 128
KD = DI // 128
NQ = 2
NSC = DS // NQ
BL = T + 1
RC = 12582912.0
NCORE = 8


def build_nc():
    ph = int(os.environ.get("KPH", "7"))
    su = int(os.environ.get("KSU", "4"))
    nc = bass.Bass("TRN2", target_bir_lowering=False,
                   detect_race_conditions=False)

    xin = nc.dram_tensor("xin", [H, L], F32, kind="ExternalInput")
    w_in = nc.dram_tensor("w_in", [H, 2 * DI], F32, kind="ExternalInput")
    w_xp = nc.dram_tensor("w_xp", [DI, 2 * DS + DI], F32, kind="ExternalInput")
    w_ou = nc.dram_tensor("w_ou", [DI, H], F32, kind="ExternalInput")
    wmod = nc.dram_tensor("wmod", [H, 6 * H], F32, kind="ExternalInput")
    alog = nc.dram_tensor("alog", [DI, DS], F32, kind="ExternalInput")
    dprm = nc.dram_tensor("dprm", [DI], F32, kind="ExternalInput")
    cnvw = nc.dram_tensor("cnvw", [DI, 4], F32, kind="ExternalInput")
    cnvb = nc.dram_tensor("cnvb", [DI], F32, kind="ExternalInput")
    nrmw = nc.dram_tensor("nrmw", [H], F32, kind="ExternalInput")
    fbia = nc.dram_tensor("fbia", [H], F32, kind="ExternalInput")
    crow = nc.dram_tensor("crow", [H], F32, kind="ExternalInput")
    wsca = nc.dram_tensor("wsca", [1, 1], F32, kind="ExternalInput")
    pack = nc.dram_tensor("pack", [256, 1024], I32, kind="ExternalInput")
    idbf = nc.dram_tensor("idbf", [128, 128], BF16, kind="ExternalInput")
    onbf = nc.dram_tensor("onbf", [128, 1], BF16, kind="ExternalInput")
    onf3 = nc.dram_tensor("onf3", [128, 1], F32, kind="ExternalInput")
    out = nc.dram_tensor("out", [H, L], F32, kind="ExternalOutput")
    dbg = nc.dram_tensor("dbg", [6, 128, KD * T], F32, kind="ExternalOutput")
    dbh = nc.dram_tensor("dbh", [6, 128, KD * T], BF16, kind="ExternalOutput")

    wi_s = nc.dram_tensor("wi_s", [8 * KH * 128 * 512], BF16)
    wxbc_s = nc.dram_tensor("wxbc_s", [KD * 128 * 32], BF16)
    wxd_s = nc.dram_tensor("wxd_s", [16 * KD * 128 * 128], BF16)
    wo_s = nc.dram_tensor("wo_s", [8 * KD * 128 * 128], BF16)
    wf_s = nc.dram_tensor("wf_s", [4 * KH * 128 * 256], BF16)
    rows1 = nc.dram_tensor("rows1", [2, T], BF16)
    rows2 = nc.dram_tensor("rows2", [1, T], BF16)
    rows3 = nc.dram_tensor("rows3", [2, T], BF16)
    bcscr = nc.dram_tensor("bcscr", [2 * DS, T], BF16)
    qsscr = nc.dram_tensor("qsscr", [T], F32)
    qiscr = nc.dram_tensor("qiscr", [T], F32)

    es = contextlib.ExitStack()
    with es:
        def sb(name, shape, dt=BF16):
            return es.enter_context(nc.sbuf_tensor(name, shape, dt))

        def psum(name, shape, dt=F32):
            return es.enter_context(nc.psum_tensor(name, shape, dt))

        xf = sb("xf", [128, KH * T], F32)
        bsq = sb("bsq", [128, 2 * T])
        hln = sb("hln", [128, KH * T])
        h1 = sb("h1", [128, KH * T])
        xss = sb("xss", [128, 2 * (T + 3)])
        halo = sb("halo", [128, KD * 3])
        cacc = sb("cacc", [128, 2 * T])
        gtmp = sb("gtmp", [128, T])
        xcv = sb("xcv", [128, KD * T])
        sz = sb("sz", [128, KD * T])
        scrA = sb("scrA", [128, 2 * T], F32)
        scrB = sb("scrB", [128, T], F32)
        dlt = sb("dlt", [128, 2 * T])
        dxs = sb("dxs", [128, 2 * T])
        Brep = sb("Brep", [128, DS * T])
        Crep = sb("Crep", [128, DS * T])
        bc16 = sb("bc16", [32, T])
        dAb = [sb(f"dA{j}", [128, NQ * BL]) for j in range(2)]
        ub = sb("ub", [128, NQ * BL])
        hb = sb("hb", [128, NQ * BL])
        yac = sb("yac", [128, T], F32)
        stt = sb("stt", [128, KD * DS], F32)
        vsc = sb("vsc", [128, T], F32)
        xnw = sb("xnw", [128, KH * T])
        h2r = sb("h2r", [128, KH * T])
        hab = sb("hab", [128, T])
        qmx = sb("qmx", [128, T])
        qcl = sb("qcl", [128, 16], F32)
        r1bc = sb("r1bc", [128, 2 * T])
        r2bc = sb("r2bc", [128, T])
        r3bc = sb("r3bc", [128, 2 * T])
        sbc = sb("sbc", [128, T], F32)
        wibc = sb("wibc", [128, T], F32)
        rowb = sb("rowb", [128, 2 * T], F32)
        rowc = sb("rowc", [128, 2 * T])
        xo = sb("xo", [128, T], F32)
        wiu = sb("wiu", [128, 2 * 4096])
        wxu = sb("wxu", [128, 2 * 2048])
        wbcu = sb("wbcu", [128, KD * 32])
        wou = sb("wou", [128, 2 * 2048])
        wfu = sb("wfu", [128, 2 * 2048])
        Ang = sb("Ang", [128, KD * DS], F32)
        Dcl = sb("Dcl", [128, KD], F32)
        cwc = sb("cwc", [128, KD * 4], F32)
        cbc = sb("cbc", [128, KD], F32)
        nwc = sb("nwc", [128, KH], F32)
        fbc = sb("fbc", [128, KH], F32)
        ca1 = sb("ca1", [128, KH], F32)
        sav = sb("sav", [128, KH], F32)
        gav = sb("gav", [128, KH], F32)
        cb1 = sb("cb1", [128, KH], F32)
        sbv = sb("sbv", [128, KH], F32)
        gbv = sb("gbv", [128, KH], F32)
        gbq = sb("gbq", [128, KH], F32)
        fb2 = sb("fb2", [128, KH], F32)
        wsc = sb("wsc", [128, 1], F32)
        crw = sb("crw", [128, KH], F32)
        slc = sb("slc", [128, KH], F32)
        ep5 = sb("ep5", [1, 1], F32)
        ep6 = sb("ep6", [1, 1], F32)
        idb = sb("idb", [128, 128])
        onb = sb("onb", [128, 1])
        onf = sb("onf", [128, 1], F32)

        psA = psum("psA", [128, T])
        psB = psum("psB", [128, T])
        pmm = [psum(f"pmm{j}", [128, T]) for j in range(4)]
        pbc = psum("pbc", [32, T])
        ptp = psum("ptp", [128, 256], BF16)

        def sem(name):
            return es.enter_context(nc.semaphore(name))

        SX = sem("SX"); SWI = sem("SWI"); SWX = sem("SWX"); SWO = sem("SWO")
        SWF = sem("SWF"); SB1 = sem("SB1"); SB2 = sem("SB2"); SBC = sem("SBC")
        SB3 = sem("SB3"); SBQ = sem("SBQ"); SOUT = sem("SOUT")
        SCST = sem("SCST"); SWLD = sem("SWLD"); SWST = sem("SWST")
        SWML = sem("SWML"); SWFS = sem("SWFS")
        t_st1 = sem("t_st1"); t_st2 = sem("t_st2"); t_st3 = sem("t_st3")
        t_win = sem("t_win"); t_wig = sem("t_wig"); t_bc = sem("t_bc")
        t_dlt = sem("t_dlt"); t_wxg = sem("t_wxg"); t_wo = sem("t_wo")
        t_wog = sem("t_wog"); t_ffn = sem("t_ffn"); t_qtp = sem("t_qtp")
        t_sqc = sem("t_sqc"); TSU1 = sem("TSU1"); TSU2 = sem("TSU2")
        v_var1 = sem("v_var1"); v_rw1 = sem("v_rw1"); v_hln = sem("v_hln")
        v_var2 = sem("v_var2"); v_h1 = sem("v_h1"); v_cnv = sem("v_cnv")
        v_bc16 = sem("v_bc16"); v_dlt = sem("v_dlt"); v_scan = sem("v_scan")
        v_v = sem("v_v"); v_xnw = sem("v_xnw"); v_var3 = sem("v_var3")
        v_rw3 = sem("v_rw3"); v_h2 = sem("v_h2"); v_qtr = sem("v_qtr")
        v_mxm = sem("v_mxm"); v_qrd = sem("v_qrd"); v_qrw = sem("v_qrw")
        v_r = sem("v_r"); v_out = sem("v_out")
        VSU1 = sem("VSU1"); VSU2 = sem("VSU2"); VSU3 = sem("VSU3")
        s_sq1 = sem("s_sq1"); s_rw1 = sem("s_rw1"); s_ev = sem("s_ev")
        s_cs = sem("s_cs"); s_zs = sem("s_zs"); s_sq2 = sem("s_sq2")
        s_rw2 = sem("s_rw2"); s_sp = sem("s_sp"); s_dA = sem("s_dA")
        s_rw3 = sem("s_rw3"); s_hab = sem("s_hab"); s_sq3 = sem("s_sq3")
        SSU1 = sem("SSU1"); SSU2 = sem("SSU2"); SSU3 = sem("SSU3")

        cast_slices = [("wi", k, 4096) for k in range(KH)] + \
                      [("wx", k, 2080) for k in range(KD)] + \
                      [("wo", k, 1024) for k in range(KD)]
        NSL = len(cast_slices)
        stores_per = [2 if kind == "wx" else 1 for kind, _, _ in cast_slices]
        stores_thru = np.cumsum([0] + stores_per)
        castA = xf[:, 0:4096]
        castB = xcv[:, :].bitcast(F32)
        cslot = [castA, castB]
        N_CST = 11

        def W(eng, s, v):
            if v > 0:
                eng.wait_ge(s, v)

        block = es.enter_context(nc.Block())

        @block.sync
        def _(sync):
            cst = [0]
            def dcst(dst, src):
                sync.dma_start(dst, src, allow_slow_non_contiguous=True
                               ).then_inc(SCST, 16)
                cst[0] += 1
            dcst(Ang[:, :].rearrange("p (g n) -> p g n", g=KD),
                 alog[:, :].rearrange("(g p) n -> p g n", p=128))
            dcst(Dcl[:, :], dprm[:].rearrange("(g p) -> p g", p=128))
            dcst(cwc[:, :].rearrange("p (g k) -> p g k", g=KD),
                 cnvw[:, :].rearrange("(g p) k -> p g k", p=128))
            dcst(cbc[:, :], cnvb[:].rearrange("(g p) -> p g", p=128))
            dcst(nwc[:, :], nrmw[:].rearrange("(g p) -> p g", p=128))
            dcst(fbc[:, :], fbia[:].rearrange("(g p) -> p g", p=128))
            dcst(idb[:, :], idbf[:, :])
            dcst(onb[:, :], onbf[:, :])
            dcst(onf[:, :], onf3[:, :])
            dcst(wsc[:, :], wsca[:, :].broadcast_to([128, 1]))
            dcst(crw[:, :], crow[:].rearrange("(g p) -> p g", p=128))
            dcst(scrA[:, 0:1024].bitcast(I32), pack[0:128, :])
            dcst(hln[:, :].bitcast(I32)[:, 0:1024], pack[128:256, :])
            assert cst[0] == N_CST + 2
            cast_list = cast_slices if su >= 2 else []
            for si, (kind, k, wid) in enumerate(cast_list):
                sl = si % 2
                if si >= 2:
                    W(sync, VSU1, si - 1)
                src = {"wi": w_in, "wx": w_xp, "wo": w_ou}[kind]
                sync.dma_start(cslot[sl][:, 0:wid],
                               src[k * 128:(k + 1) * 128, :]).then_inc(SWLD, 16)
                W(sync, VSU1, si + 1)
                cap = wiu[:, sl * 4096: sl * 4096 + wid]
                if kind == "wi":
                    sync.dma_start(
                        bass.AP(wi_s, k * 128 * 512,
                                [[512, 128], [KH * 128 * 512, 8], [1, 512]]),
                        cap.rearrange("p (g m) -> p g m", g=8),
                    ).then_inc(SWST, 16)
                elif kind == "wx":
                    sync.dma_start(
                        bass.AP(wxbc_s, k * 128 * 32, [[32, 128], [1, 32]]),
                        cap[:, 0:32],
                    ).then_inc(SWST, 16)
                    sync.dma_start(
                        bass.AP(wxd_s, k * 128 * 128,
                                [[128, 128], [KD * 128 * 128, 16], [1, 128]]),
                        cap[:, 32:2080].rearrange("p (g m) -> p g m", g=16),
                    ).then_inc(SWST, 16)
                else:
                    sync.dma_start(
                        bass.AP(wo_s, k * 128 * 128,
                                [[128, 128], [KD * 128 * 128, 8], [1, 128]]),
                        cap.rearrange("p (g m) -> p g m", g=8),
                    ).then_inc(SWST, 16)
            if su >= 2:
                W(sync, VSU1, NSL)
            for mc in range(48 if su >= 3 else 0):
                if mc >= 2:
                    W(sync, TSU1, mc - 1)
                sync.dma_start(
                    cslot[mc % 2][:, 0:1024].rearrange("p (g m) -> p g m", g=KH),
                    bass.AP(wmod, mc * 128,
                            [[6 * H, 128], [128 * 6 * H, KH], [1, 128]]),
                ).then_inc(SWML, 16)
            if su >= 4:
                W(sync, SSU2, 64)
            for u in range(4 if su >= 4 else 0):
                sync.dma_start(
                    bass.AP(wf_s, u * KH * 128 * 256,
                            [[256, 128], [128 * 256, KH], [1, 256]]),
                    Brep[:, :].rearrange("p (kc o) -> p kc o", kc=KH)
                    [:, :, u * 256:(u + 1) * 256],
                ).then_inc(SWFS, 16)

            if su >= 3:
                W(sync, TSU1, 48)
            if su >= 2:
                W(sync, SWST, 16 * int(stores_thru[NSL]))
            def sy_x(j):
                if ph >= 5:
                    W(sync, v_xnw, 8 * j)
                else:
                    W(sync, v_hln, 8 * j)
                sync.dma_start(
                    xf[:, :].rearrange("p (g t) -> p g t", g=KH),
                    xin[:, j * T:(j + 1) * T].rearrange("(g p) t -> p g t",
                                                        p=128),
                ).then_inc(SX, 16)

            def sy_rows1(j):
                W(sync, v_rw1, j + 1)
                sync.dma_start(rows1[0:1, :], rowc[0:1, 0:T]).then_inc(SB1, 16)
                sync.dma_start(rows1[1:2, :], rowc[0:1, T:2 * T]
                               ).then_inc(SB1, 16)
                W(sync, SB1, 16 * (3 * j + 2))
                sync.dma_start(
                    r1bc[:, :].rearrange("p (a t) -> p a t", a=2),
                    rows1[:, :][None, :, :].broadcast_to([128, 2, T]),
                ).then_inc(SB1, 16)

            def sy_rows2(j):
                W(sync, s_rw2, j + 1)
                sync.dma_start(rows2[:, :], rowc[0:1, 0:T]).then_inc(SB2, 16)
                W(sync, SB2, 16 * (2 * j + 1))
                sync.dma_start(
                    r2bc[:, :], rows2[0, :][None, :].broadcast_to([128, T]),
                ).then_inc(SB2, 16)

            def sy_rows3(j):
                W(sync, v_rw3, j + 1)
                sync.dma_start(rows3[0:1, :], rowc[0:1, 0:T]).then_inc(SB3, 16)
                sync.dma_start(rows3[1:2, :], rowc[0:1, T:2 * T]
                               ).then_inc(SB3, 16)
                W(sync, SB3, 16 * (3 * j + 2))
                sync.dma_start(
                    r3bc[:, :].rearrange("p (a t) -> p a t", a=2),
                    rows3[:, :][None, :, :].broadcast_to([128, 2, T]),
                ).then_inc(SB3, 16)

            def sy_quant(j):
                W(sync, v_qrw, j + 1)
                sync.dma_start(bass.AP(qsscr, 0, [[1, 128], [128, 4]]),
                               qcl[:, 0:4], allow_slow_non_contiguous=True
                               ).then_inc(SBQ, 16)
                W(sync, SBQ, 16 * (2 * j + 1))
                sync.dma_start(
                    wibc[:, :], qsscr[:][None, :].broadcast_to([128, T]),
                ).then_inc(SBQ, 16)

            def sy_win(j, us=range(8)):
                for u in us:
                    if u >= 2 or j > 0:
                        W(sync, t_win, 32 * j + 4 * u - 4)
                    sync.dma_start(
                        wiu[:, (u % 2) * 4096:(u % 2) * 4096 + 4096]
                        .rearrange("p (g m) -> p g m", g=8),
                        bass.AP(wi_s, u * KH * 128 * 512,
                                [[512, 128], [128 * 512, 8], [1, 512]]),
                    ).then_inc(SWI, 16)

            def sy_bcw(j):
                W(sync, t_bc, j)
                sync.dma_start(
                    wbcu[:, :].rearrange("p (g m) -> p g m", g=KD),
                    bass.AP(wxbc_s, 0, [[32, 128], [128 * 32, 16], [1, 32]]),
                ).then_inc(SWX, 16)

            def sy_bcb(j):
                W(sync, v_bc16, j + 1)
                sync.dma_start(bcscr[:, :], bc16[:, :]).then_inc(SBC, 16)
                W(sync, SBC, 16 * (5 * j + 1))
                if j == 0:
                    W(sync, SWFS, 64)
                if ph >= 4:
                    W(sync, v_v, 16 * j)
                HS = DS // 2
                for hf in range(2):
                    sync.dma_start(
                        Brep[:, hf * HS * T:(hf + 1) * HS * T]
                        .rearrange("p (n t) -> p n t", n=HS),
                        bcscr[hf * HS:(hf + 1) * HS, :][None, :, :]
                        .broadcast_to([128, HS, T]),
                    ).then_inc(SBC, 16)
                    sync.dma_start(
                        Crep[:, hf * HS * T:(hf + 1) * HS * T]
                        .rearrange("p (n t) -> p n t", n=HS),
                        bcscr[DS + hf * HS:DS + (hf + 1) * HS, :][None, :, :]
                        .broadcast_to([128, HS, T]),
                    ).then_inc(SBC, 16)

            def sy_dltw(j, us=range(16)):
                for u in us:
                    W(sync, t_dlt, 16 * j + u - 1)
                    sync.dma_start(
                        wxu[:, (u % 2) * 2048:(u % 2) * 2048 + 2048]
                        .rearrange("p (g m) -> p g m", g=KD),
                        bass.AP(wxd_s, u * KD * 128 * 128,
                                [[128, 128], [128 * 128, 16], [1, 128]]),
                    ).then_inc(SWX, 16)

            def sy_wow(j):
                for u in range(8):
                    W(sync, t_wo, 8 * j + u - 1)
                    sync.dma_start(
                        wou[:, (u % 2) * 2048:(u % 2) * 2048 + 2048]
                        .rearrange("p (g m) -> p g m", g=KD),
                        bass.AP(wo_s, u * KD * 128 * 128,
                                [[128, 128], [128 * 128, 16], [1, 128]]),
                    ).then_inc(SWO, 16)

            def sy_ffnout(j):
                for mc in range(KH):
                    if mc % 2 == 0:
                        u = mc // 2
                        W(sync, t_ffn, 8 * j + 2 * u - 2)
                        sync.dma_start(
                            wfu[:, (u % 2) * 2048:(u % 2) * 2048 + 2048]
                            .rearrange("p (g m) -> p g m", g=KH),
                            bass.AP(wf_s, u * KH * 128 * 256,
                                    [[256, 128], [128 * 256, 8], [1, 256]]),
                        ).then_inc(SWF, 16)
                    W(sync, v_out, 8 * j + mc + 1)
                    sync.dma_start(
                        out[mc * 128:(mc + 1) * 128, j * T:(j + 1) * T],
                        xo[:, :],
                    ).then_inc(SOUT, 16)

            if ph >= 1:
                sy_x(0)
                sy_rows1(0)
                sy_rows2(0)
            if ph >= 2:
                sy_win(0, range(6))
            if ph >= 3:
                sy_bcw(0)
                sy_bcb(0)
                sy_dltw(0, range(6))
            if ph >= 2:
                sy_win(0, range(6, 7))
            if ph >= 3:
                sy_dltw(0, range(6, 10))
            if ph >= 2:
                sy_win(0, range(7, 8))
            if ph >= 3:
                sy_dltw(0, range(10, 16))
            if ph >= 5:
                sy_wow(0)
            for i in range(NT):
                if ph < 1:
                    break
                if ph >= 6:
                    sy_rows3(i)
                if i + 1 < NT:
                    sy_x(i + 1)
                    sy_rows1(i + 1)
                if ph >= 6:
                    sy_quant(i)
                if i + 1 < NT and ph >= 2:
                    sy_win(i + 1, range(2))
                if ph >= 7:
                    sy_ffnout(i)
                if i + 1 < NT:
                    sy_rows2(i + 1)
                if i + 1 < NT:
                    if ph >= 2:
                        sy_win(i + 1, range(2, 6))
                    if ph >= 3:
                        sy_bcw(i + 1)
                        sy_bcb(i + 1)
                        sy_dltw(i + 1, range(6))
                    if ph >= 2:
                        sy_win(i + 1, range(6, 7))
                    if ph >= 3:
                        sy_dltw(i + 1, range(6, 10))
                    if ph >= 2:
                        sy_win(i + 1, range(7, 8))
                    if ph >= 3:
                        sy_dltw(i + 1, range(10, 16))
                    if ph >= 5:
                        sy_wow(i + 1)
            if ph >= 7:
                sync.wait_ge(SOUT, 16 * 8 * NT)
                if int(os.environ.get("KDBG", "0")):
                    W(sync, v_var1, NT + 1)
                    nd = [0]
                    def dmp(j, ap):
                        sync.dma_start(dbg[j, :, 0:ap.shape[1]], ap
                                       ).then_inc(SCST, 16)
                        nd[0] += 1
                    dmp(0, stt[:, :])
                    dmp(1, scrB[:, :])
                    dmp(2, yac[:, :])
                    dmp(3, vsc[:, :])
                    for gi, gap in enumerate((sav, ca1, gav, sbv, cb1,
                                              gbv, fb2)):
                        sync.dma_start(dbg[4, :, gi * 8:gi * 8 + 8],
                                       gap[:, :]).then_inc(SCST, 16)
                        nd[0] += 1
                    def dmh(j, ap):
                        sync.dma_start(dbh[j, :, 0:ap.shape[1]], ap
                                       ).then_inc(SCST, 16)
                        nd[0] += 1
                    dmh(0, hln[:, :])
                    dmh(1, h1[:, :])
                    dmh(2, xcv[:, :])
                    dmh(3, sz[:, :])
                    dmh(4, xnw[:, :])
                    dmh(5, h2r[:, :])
                    sync.wait_ge(SCST, 16 * (N_CST + 2 + nd[0]))
            elif ph == 6:
                sync.wait_ge(v_r, 8 * NT)
            elif ph == 5:
                sync.wait_ge(v_xnw, 8 * NT)
            elif ph == 4:
                sync.wait_ge(v_v, 16 * NT)
            elif ph == 3:
                sync.wait_ge(t_dlt, 16 * NT)
            elif ph == 2:
                sync.wait_ge(s_zs, 16 * NT)
            elif ph == 1:
                sync.wait_ge(v_h1, 8 * NT)
            else:
                if su >= 3:
                    sync.wait_ge(VSU3, 1)
                if su >= 4:
                    sync.wait_ge(SSU2, 64)
                    sync.wait_ge(SWFS, 64)
                if su == 2:
                    sync.wait_ge(VSU1, NSL)
                sync.wait_ge(SSU3, 1)

        @block.tensor
        def _(tensor):
            if su >= 3:
                W(tensor, SSU1, 1)
            for mc in range(48 if su >= 3 else 0):
                W(tensor, SWML, 16 * (mc + 1))
                for k in range(KH):
                    mm = tensor.matmul(
                        psA[:, mc:mc + 1],
                        cslot[mc % 2][:, k * 128:(k + 1) * 128],
                        slc[:, k:k + 1],
                        start=(k == 0), stop=(k == KH - 1))
                mm.then_inc(TSU1, 1)
            if su >= 4:
                W(tensor, VSU2, 1)
            for j in range(64 if su >= 4 else 0):
                W(tensor, SSU2, j)
                tensor.transpose(
                    ptp[:, (j % 2) * 128:(j % 2) * 128 + 128],
                    sz[:, j * 128:(j + 1) * 128],
                    idb[:, :],
                ).then_inc(TSU2, 1)

            W(tensor, SCST, 16 * (N_CST + 2))
            W(tensor, VSU3, 1)

            def pe_stats1(j):
                W(tensor, SX, 16 * (j + 1))
                if ph >= 6:
                    W(tensor, v_var3, j)
                for kc in range(KH):
                    tensor.matmul(psA[0:1, :], onf[:, :],
                                  xf[:, kc * T:(kc + 1) * T],
                                  start=(kc == 0), stop=(kc == KH - 1))
                for kc in range(KH):
                    W(tensor, s_sq1, 8 * j + kc + 1)
                    mm = tensor.matmul(psB[0:1, :], onb[:, :],
                                       bsq[:, (kc % 2) * T:(kc % 2) * T + T],
                                       start=(kc == 0), stop=(kc == KH - 1))
                    mm.then_inc(t_sqc, 1)

            def pe_stats2(j):
                W(tensor, v_var1, j + 1)
                for kc in range(KH):
                    W(tensor, s_sq2, 8 * j + kc + 1)
                    mm = tensor.matmul(psA[0:1, :], onb[:, :],
                                       bsq[:, (kc % 2) * T:(kc % 2) * T + T],
                                       start=(kc == 0), stop=(kc == KH - 1))
                    mm.then_inc(t_sqc, 1)

            def pe_win(j):
                W(tensor, v_h1, 8 * (j + 1))
                for g in range(4):
                    W(tensor, SWI, 16 * (8 * j + g + 1))
                    for ml in range(4):
                        mc = 4 * g + ml
                        W(tensor, s_ev, 16 * j + mc - 3)
                        for kc in range(KH):
                            mm = tensor.matmul(
                                pmm[ml][:, :],
                                wiu[:, (g % 2) * 4096 + kc * 512 + ml * 128:
                                    (g % 2) * 4096 + kc * 512 + (ml + 1) * 128],
                                h1[:, kc * T:(kc + 1) * T],
                                start=(kc == 0), stop=(kc == KH - 1))
                        mm.then_inc(t_win, 1)


            def pe_bc(j):
                W(tensor, SWX, 16 * (17 * j + 1))
                W(tensor, v_bc16, j)
                for kc in range(KD):
                    W(tensor, s_cs, 16 * j + kc + 1)
                    mm = tensor.matmul(pbc[:, :],
                                       wbcu[:, kc * 32:(kc + 1) * 32],
                                       xcv[:, kc * T:(kc + 1) * T],
                                       start=(kc == 0), stop=(kc == KD - 1))
                mm.then_inc(t_bc, 1)

            def pe_zmm(j, dg):
                g = dg // 4
                if dg % 4 == 0:
                    W(tensor, SWI, 16 * (8 * j + 4 + g + 1))
                if dg == 0:
                    W(tensor, v_xnw, 8 * j)
                else:
                    W(tensor, s_zs, 16 * j + dg)
                ml = dg % 4
                for kc in range(KH):
                    mm = tensor.matmul(
                        pmm[3][:, :],
                        wiu[:, ((4 + g) % 2) * 4096 + kc * 512 + ml * 128:
                            ((4 + g) % 2) * 4096 + kc * 512 + (ml + 1) * 128],
                        h1[:, kc * T:(kc + 1) * T],
                        start=(kc == 0), stop=(kc == KH - 1))
                mm.then_inc(t_win, 1)

            def pe_delta(j):
                if ph >= 7:
                    W(tensor, v_out, 8 * j)
                for dg in range(KD):
                    W(tensor, SWX, 16 * (17 * j + dg + 2))
                    if dg >= 2 and ph >= 4:
                        W(tensor, s_sp, 16 * j + dg - 1)
                    for kc in range(KD):
                        if dg == 0:
                            W(tensor, s_cs, 16 * j + kc + 1)
                        mm = tensor.matmul(
                            pmm[dg % 2][:, :],
                            wxu[:, (dg % 2) * 2048 + kc * 128:
                                (dg % 2) * 2048 + (kc + 1) * 128],
                            xcv[:, kc * T:(kc + 1) * T],
                            start=(kc == 0), stop=(kc == KD - 1))
                    mm.then_inc(t_dlt, 1)
                    if ph >= 2:
                        pe_zmm(j, dg)

            def pe_wout(j):
                W(tensor, v_v, 16 * (j + 1))
                for mc in range(KH):
                    W(tensor, SWO, 16 * (8 * j + mc + 1))
                    W(tensor, v_xnw, 8 * j + mc - 1)
                    for kc in range(KD):
                        mm = tensor.matmul(
                            pmm[2 + mc % 2][:, :],
                            wou[:, (mc % 2) * 2048 + kc * 128:
                                (mc % 2) * 2048 + (kc + 1) * 128],
                            sz[:, kc * T:(kc + 1) * T],
                            start=(kc == 0), stop=(kc == KD - 1))
                    mm.then_inc(t_wo, 1)

            def pe_ln2(j):
                W(tensor, v_xnw, 8 * (j + 1))
                W(tensor, v_var2, j + 1)
                for kc in range(KH):
                    tensor.matmul(psA[0:1, :], onb[:, :],
                                  xnw[:, kc * T:(kc + 1) * T],
                                  start=(kc == 0), stop=(kc == KH - 1))
                for kc in range(KH):
                    W(tensor, s_sq3, 8 * j + kc + 1)
                    mm = tensor.matmul(psB[0:1, :], onb[:, :],
                                       bsq[:, (kc % 2) * T:(kc % 2) * T + T],
                                       start=(kc == 0), stop=(kc == KH - 1))
                    mm.then_inc(t_sqc, 1)

            def pe_qtp(j):
                W(tensor, v_qtr, 8 * (j + 1))
                for jj in range(4):
                    W(tensor, v_qrd, 4 * j + jj)
                    tensor.transpose(ptp[:, (jj % 2) * 128:(jj % 2) * 128 + 128],
                                     qmx[:, jj * 128:(jj + 1) * 128],
                                     idb[:, :]).then_inc(t_qtp, 1)

            def pe_ffn(j):
                W(tensor, v_r, 8 * (j + 1))
                for mc in range(KH):
                    u = mc // 2
                    W(tensor, SWF, 16 * (4 * j + u + 1))
                    W(tensor, v_out, 8 * j + mc - 1)
                    for kc in range(KH):
                        mm = tensor.matmul(
                            pmm[mc % 2][:, :],
                            wfu[:, (u % 2) * 2048 + kc * 256 + (mc % 2) * 128:
                                (u % 2) * 2048 + kc * 256 + (mc % 2) * 128 + 128],
                            h2r[:, kc * T:(kc + 1) * T],
                            start=(kc == 0), stop=(kc == KH - 1))
                    mm.then_inc(t_ffn, 1)

            if ph >= 1:
                pe_stats1(0)
            if ph >= 1:
                pe_stats2(0)
            for i in range(NT):
                if ph < 1:
                    break
                if ph >= 2:
                    pe_win(i)
                if ph >= 3:
                    pe_bc(i)
                    pe_delta(i)
                else:
                    for dg in range(KD if ph >= 2 else 0):
                        pe_zmm(i, dg)
                if ph >= 5:
                    pe_wout(i)
                if ph >= 6:
                    pe_ln2(i)
                if i + 1 < NT:
                    pe_stats1(i + 1)
                if ph >= 6:
                    pe_qtp(i)
                if i + 1 < NT:
                    pe_stats2(i + 1)
                if ph >= 7:
                    pe_ffn(i)

        @block.vector
        def _(vector):
            vector.memset(stt[:, :], 0.0)
            vector.memset(halo[:, :], 0.0)
            vector.memset(ep5[:, :], 1e-5)
            vector.memset(ep6[:, :], 1e-6)
            for j in range(2):
                for q in range(NQ):
                    vector.memset(dAb[j][:, q * BL:q * BL + 1], 0.0)
            for si, (kind, k, wid) in enumerate(cast_slices if su >= 2 else []):
                sl = si % 2
                W(vector, SWLD, 16 * (si + 1))
                if si >= 2:
                    W(vector, SWST, 16 * int(stores_thru[si - 1]))
                vector.tensor_copy(wiu[:, sl * 4096: sl * 4096 + wid],
                                   cslot[sl][:, 0:wid]).then_inc(VSU1, 1)
            if su < 4:
                vector.engine_nop().then_inc(VSU2, 1)
            else:
                W(vector, SCST, 16 * (N_CST + 2))
                ti = bsq[:, :].bitcast(I32)
                for s4 in range(4):
                    for ot in range(2):
                        srcw = (scrA[:, 0:1024].bitcast(I32) if ot == 0
                                else hln[:, :].bitcast(I32)[:, 0:1024])
                        for hf in range(2):
                            sw = srcw[:, hf * 512:(hf + 1) * 512]
                            if s4 == 0:
                                vector.tensor_scalar(ti[:, 0:512], sw, 3, None,
                                                     ALU.bitwise_and)
                            else:
                                vector.tensor_scalar(ti[:, 0:512], sw, 2 * s4,
                                                     None, ALU.logical_shift_right)
                                vector.tensor_scalar(ti[:, 0:512], ti[:, 0:512],
                                                     3, None, ALU.bitwise_and)
                            vector.tensor_scalar(ti[:, 0:512], ti[:, 0:512], 1,
                                                 None, ALU.subtract)
                            vector.tensor_copy(
                                sz[:, (s4 * 2 + ot) * 1024 + hf * 512:
                                   (s4 * 2 + ot) * 1024 + (hf + 1) * 512],
                                ti[:, 0:512])
                vector.engine_nop().then_inc(VSU2, 1)
            if su < 3:
                vector.engine_nop().then_inc(VSU3, 1)
            else:
                W(vector, TSU1, 48)
                vector.tensor_copy(sav[:, :], psA[:, 0:8])
                vector.tensor_scalar(ca1[:, :], psA[:, 8:16], 1.0, None, ALU.add)
                vector.tensor_copy(sbv[:, :], psA[:, 24:32])
                vector.tensor_scalar(cb1[:, :], psA[:, 32:40], 1.0, None, ALU.add)
                g = scrA[:, 0 * 64:0 * 64 + 48]
                a = scrA[:, 1 * 64:1 * 64 + 48]
                b = scrA[:, 2 * 64:2 * 64 + 48]
                m = scrA[:, 3 * 64:3 * 64 + 48]
                d = scrA[:, 4 * 64:4 * 64 + 48]
                e = scrA[:, 5 * 64:5 * 64 + 48]
                r = scrA[:, 6 * 64:6 * 64 + 48]
                vector.tensor_copy(g, psA[:, 0:48])
                vector.tensor_scalar(a, g, 0.25, None, ALU.max)
                vector.tensor_scalar(b, g, -0.25, None, ALU.min)
                vector.tensor_scalar(m, g, 0.0, None, ALU.is_ge)
                vector.tensor_tensor(d, a, b, ALU.subtract)
                vector.tensor_tensor(e, m, d, ALU.mult)
                vector.tensor_tensor(r, b, e, ALU.add)
                vector.tensor_copy(gav[:, :], r[:, 16:24])
                vector.tensor_copy(gbv[:, :], r[:, 40:48])
                vector.tensor_scalar(gbq[:, :], r[:, 40:48], wsc[:, 0:1],
                                     1.0 / 127.0, ALU.mult, ALU.mult)
                vector.tensor_tensor(fb2[:, :], fbc[:, :], r[:, 40:48],
                                     ALU.mult).then_inc(VSU3, 1)

            def dv_rows1(j):
                W(vector, t_sqc, 24 * j + 8)
                W(vector, s_rw3, j)
                vector.tensor_scalar(rowb[0:1, 0:T], psA[0:1, :], 1.0 / H,
                                     None, ALU.mult)
                vector.tensor_tensor(rowb[0:1, T:2 * T], rowb[0:1, 0:T],
                                     rowb[0:1, 0:T], ALU.mult)
                vector.scalar_tensor_tensor(rowb[0:1, T:2 * T], psB[0:1, :],
                                            1.0 / H, rowb[0:1, T:2 * T],
                                            ALU.mult, ALU.subtract
                                            ).then_inc(v_var1, 1)
                W(vector, s_rw1, j + 1)
                W(vector, SB3, 16 * (3 * j - 1))
                vector.scalar_tensor_tensor(rowc[0:1, T:2 * T],
                                            rowb[0:1, 0:T], -1.0,
                                            rowc[0:1, 0:T],
                                            ALU.mult, ALU.mult
                                            ).then_inc(v_rw1, 1)

            def dv_apply1(j):
                W(vector, SB1, 16 * (3 * j + 3))
                for kc in range(KH):
                    sA = scrA[:, (kc % 2) * T:(kc % 2 + 1) * T]
                    vector.tensor_tensor(sA, xf[:, kc * T:(kc + 1) * T],
                                         r1bc[:, 0:T], ALU.mult)
                    vector.tensor_tensor(sA, sA, r1bc[:, T:2 * T], ALU.add)
                    vector.tensor_scalar(hln[:, kc * T:(kc + 1) * T], sA,
                                         ca1[:, kc:kc + 1], sav[:, kc:kc + 1],
                                         ALU.mult, ALU.add).then_inc(v_hln, 1)

            def dv_rmsrow(j):
                W(vector, t_sqc, 24 * j + 16)
                vector.tensor_scalar(rowb[0:1, T:2 * T], psA[0:1, :], 1.0 / H,
                                     None, ALU.mult).then_inc(v_var2, 1)

            def dv_h1(j):
                W(vector, SB2, 16 * (2 * j + 2))
                if ph >= 2:
                    W(vector, t_win, 32 * j)
                for kc in range(KH):
                    vector.scalar_tensor_tensor(
                        h1[:, kc * T:(kc + 1) * T],
                        hln[:, kc * T:(kc + 1) * T],
                        nwc[:, kc:kc + 1], r2bc[:, :],
                        ALU.mult, ALU.mult).then_inc(v_h1, 1)

            def dv_conv(j):
                for dg in range(KD):
                    W(vector, s_ev, 16 * j + dg + 1)
                    W(vector, s_cs, 16 * j + dg - 1)
                    x0 = (dg % 2) * (T + 3)
                    ca = cacc[:, (dg % 2) * T:(dg % 2 + 1) * T]
                    vector.tensor_scalar(ca, xss[:, x0:x0 + T],
                                         cwc[:, dg * 4:dg * 4 + 1], None,
                                         ALU.mult)
                    for kk in range(1, 3):
                        vector.scalar_tensor_tensor(
                            ca, xss[:, x0 + kk:x0 + kk + T],
                            cwc[:, dg * 4 + kk:dg * 4 + kk + 1], ca,
                            ALU.mult, ALU.add)
                    vector.scalar_tensor_tensor(
                        ca, xss[:, x0 + 3:x0 + 3 + T],
                        cwc[:, dg * 4 + 3:dg * 4 + 4], ca,
                        ALU.mult, ALU.add).then_inc(v_cnv, 1)

            def dv_scan(j):
                for dg in range(KD):
                    W(vector, s_sp, 16 * j + dg + 1)
                    if dg >= 2:
                        W(vector, s_dA, 8 * KD * j + 8 * dg - 8)
                    dsl = dlt[:, (dg % 2) * T:(dg % 2 + 1) * T]
                    vector.tensor_tensor(dsl, scrA[:, (dg % 2) * T:
                                                    (dg % 2 + 1) * T],
                                         scrB[:, :], ALU.add)
                    vector.tensor_tensor(dxs[:, (dg % 2) * T:
                                              (dg % 2 + 1) * T], dsl,
                                         xcv[:, dg * T:(dg + 1) * T], ALU.mult
                                         ).then_inc(v_dlt, 1)
                    for q in range(NSC):
                        if dg == 0 and q == 0:
                            W(vector, SBC, 16 * (5 * j + 3))
                        if dg == 0 and q == NSC // 2:
                            W(vector, SBC, 16 * (5 * j + 5))
                        st0 = dg * DS + NQ * q
                        ubv = ub[:, :].rearrange("p (q b) -> p q b", q=NQ)
                        hbv = hb[:, :].rearrange("p (q b) -> p q b", q=NQ)
                        for blk in range(NQ):
                            vector.tensor_copy(
                                ub[:, blk * BL:blk * BL + 1],
                                stt[:, st0 + blk:st0 + blk + 1])
                        vector.tensor_tensor(
                            ubv[:, :, 1:BL],
                            dxs[:, (dg % 2) * T:(dg % 2 + 1) * T]
                            [:, None, :].broadcast_to([128, NQ, T]),
                            Brep[:, :].rearrange("p (n t) -> p n t", n=DS)
                            [:, NQ * q:NQ * q + NQ, :],
                            ALU.mult)
                        W(vector, s_dA, 8 * KD * j + 8 * dg + q + 1)
                        vector.tensor_tensor_scan(
                            hb[:, :], dAb[(NSC * dg + q) % 2][:, :], ub[:, :],
                            0.0, ALU.mult, ALU.add).then_inc(v_scan, 1)
                        for blk in range(NQ):
                            vector.tensor_copy(
                                stt[:, st0 + blk:st0 + blk + 1],
                                hb[:, blk * BL + BL - 1:blk * BL + BL])
                        vector.tensor_tensor(
                            ubv[:, :, 1:BL], hbv[:, :, 1:BL],
                            Crep[:, :].rearrange("p (n t) -> p n t", n=DS)
                            [:, NQ * q:NQ * q + NQ, :],
                            ALU.mult)
                        if q == 0:
                            vector.tensor_tensor(yac[:, :], ubv[:, 0, 1:BL],
                                                 ubv[:, 1, 1:BL], ALU.add)
                        else:
                            vector.tensor_tensor(yac[:, :], yac[:, :],
                                                 ubv[:, 0, 1:BL], ALU.add)
                            vector.tensor_tensor(yac[:, :], yac[:, :],
                                                 ubv[:, 1, 1:BL], ALU.add)
                    W(vector, s_zs, 16 * j + dg + 1)
                    vector.scalar_tensor_tensor(
                        vsc[:, :], xcv[:, dg * T:(dg + 1) * T],
                        Dcl[:, dg:dg + 1], yac[:, :], ALU.mult, ALU.add)
                    vector.tensor_tensor(sz[:, dg * T:(dg + 1) * T],
                                         vsc[:, :], sz[:, dg * T:(dg + 1) * T],
                                         ALU.mult).then_inc(v_v, 1)

            def dv_wout(j):
                for mc in range(KH):
                    W(vector, t_wo, 8 * j + mc + 1)
                    esl = scrA[:, (mc % 2) * T:(mc % 2 + 1) * T]
                    vector.tensor_tensor(esl, pmm[2 + mc % 2][:, :],
                                         hln[:, mc * T:(mc + 1) * T], ALU.add)
                    vector.scalar_tensor_tensor(
                        xnw[:, mc * T:(mc + 1) * T], esl,
                        gav[:, mc:mc + 1], xf[:, mc * T:(mc + 1) * T],
                        ALU.mult, ALU.add).then_inc(v_xnw, 1)

            def dv_rows2(j):
                W(vector, t_sqc, 24 * j + 24)
                vector.tensor_scalar(rowb[0:1, 0:T], psA[0:1, :], 1.0 / H,
                                     None, ALU.mult)
                vector.tensor_tensor(rowb[0:1, T:2 * T], rowb[0:1, 0:T],
                                     rowb[0:1, 0:T], ALU.mult)
                vector.scalar_tensor_tensor(rowb[0:1, T:2 * T], psB[0:1, :],
                                            1.0 / H, rowb[0:1, T:2 * T],
                                            ALU.mult, ALU.subtract
                                            ).then_inc(v_var3, 1)
                W(vector, s_rw3, j + 1)
                vector.scalar_tensor_tensor(rowc[0:1, T:2 * T],
                                            rowb[0:1, 0:T], -1.0,
                                            rowc[0:1, 0:T],
                                            ALU.mult, ALU.mult
                                            ).then_inc(v_rw3, 1)

            def dv_h2(j):
                W(vector, SB3, 16 * (3 * j + 3))
                for kc in range(KH):
                    sA = scrA[:, (kc % 2) * T:(kc % 2 + 1) * T]
                    vector.tensor_tensor(sA, xnw[:, kc * T:(kc + 1) * T],
                                         r3bc[:, 0:T], ALU.mult)
                    vector.tensor_tensor(sA, sA, r3bc[:, T:2 * T], ALU.add)
                    vector.tensor_scalar(h2r[:, kc * T:(kc + 1) * T], sA,
                                         cb1[:, kc:kc + 1], sbv[:, kc:kc + 1],
                                         ALU.mult, ALU.add).then_inc(v_h2, 1)

            def dv_qmx(j):
                W(vector, t_qtp, 4 * j)
                for kc in range(KH):
                    W(vector, s_hab, 8 * j + kc + 1)
                    if kc == 0:
                        vector.tensor_copy(qmx[:, :], hab[:, :]
                                           ).then_inc(v_qtr, 1)
                    else:
                        vector.tensor_tensor(qmx[:, :], qmx[:, :],
                                             hab[:, :], ALU.max
                                             ).then_inc(v_qtr, 1)

            def dv_qred(j):
                for jj in range(4):
                    W(vector, t_qtp, 4 * j + jj + 1)
                    vector.reduce_max(qcl[:, jj:jj + 1],
                                      ptp[:, (jj % 2) * 128:(jj % 2) * 128 + 128],
                                      AX.X).then_inc(v_qrd, 1)
                vector.engine_nop().then_inc(v_qrw, 1)

            def dv_scale_r(j):
                W(vector, SBQ, 16 * (2 * j + 2))
                vector.tensor_scalar(wibc[:, :], wibc[:, :], 1e-5, None,
                                     ALU.max)
                vector.reciprocal(sbc[:, :], wibc[:, :])
                vector.tensor_scalar(sbc[:, :], sbc[:, :], 127.0, None,
                                     ALU.mult)
                for kc in range(KH):
                    sA = scrA[:, (kc % 2) * T:(kc % 2 + 1) * T]
                    vector.tensor_tensor(sA, h2r[:, kc * T:(kc + 1) * T],
                                         sbc[:, :], ALU.mult)
                    vector.tensor_scalar(h2r[:, kc * T:(kc + 1) * T], sA,
                                         RC, RC, ALU.add, ALU.subtract
                                         ).then_inc(v_r, 1)

            def dv_ffnout(j):
                for mc in range(KH):
                    W(vector, t_ffn, 8 * j + mc + 1)
                    sA = scrA[:, (mc % 2) * T:(mc % 2 + 1) * T]
                    vector.tensor_tensor(sA, pmm[mc % 2][:, :], wibc[:, :],
                                         ALU.mult)
                    vector.tensor_scalar(scrB[:, :], sA,
                                         gbq[:, mc:mc + 1], fb2[:, mc:mc + 1],
                                         ALU.mult, ALU.add)
                    W(vector, SOUT, 16 * (8 * j + mc))
                    vector.tensor_tensor(xo[:, :], scrB[:, :],
                                         xnw[:, mc * T:(mc + 1) * T], ALU.add
                                         ).then_inc(v_out, 1)

            if ph >= 1:
                dv_rows1(0)
                dv_apply1(0)
                dv_rmsrow(0)
                dv_h1(0)
            if ph >= 2:
                dv_conv(0)
            for i in range(NT):
                if ph < 1:
                    break
                if ph >= 4:
                    dv_scan(i)
                if ph >= 5:
                    dv_wout(i)
                if ph >= 6:
                    dv_rows2(i)
                if i + 1 < NT:
                    dv_rows1(i + 1)
                if ph >= 6:
                    dv_h2(i)
                    dv_qmx(i)
                    dv_qred(i)
                if i + 1 < NT:
                    dv_apply1(i + 1)
                if ph >= 6:
                    dv_scale_r(i)
                if i + 1 < NT:
                    dv_rmsrow(i + 1)
                if ph >= 7:
                    dv_ffnout(i)
                if i + 1 < NT:
                    dv_h1(i + 1)
                    if ph >= 2:
                        dv_conv(i + 1)

            if int(os.environ.get("KDBG", "0")) and ph >= 7:
                W(vector, SOUT, 16 * 8 * NT)
                vector.tensor_copy(scrB[:, :], pmm[2][:, :])
                vector.tensor_copy(vsc[:, :], pmm[3][:, :]).then_inc(v_var1, 1)

        @block.scalar
        def _(scalar):
            W(scalar, SCST, 16 * (N_CST + 2))
            scalar.activation(slc[:, :], crw[:, :], AF.Silu
                              ).then_inc(SSU1, 1)
            scalar.activation(Ang[:, :], Ang[:, :], AF.Exp)
            scalar.activation(Ang[:, :], Ang[:, :], AF.Copy, bias=0.0,
                              scale=-1.0).then_inc(SSU3, 1)
            for j in range(64 if su >= 4 else 0):
                W(scalar, TSU2, j + 1)
                s4, rem = divmod(j, 2 * KH)
                ot, ib = divmod(rem, KH)
                o0 = s4 * 256 + ot * 128
                scalar.copy(Brep[:, ib * H + o0: ib * H + o0 + 128],
                            ptp[:, (j % 2) * 128:(j % 2) * 128 + 128]
                            ).then_inc(SSU2, 1)

            def ac_sq1(j):
                W(scalar, SX, 16 * (j + 1))
                for kc in range(KH):
                    W(scalar, t_sqc, 24 * j + kc - 1)
                    scalar.activation(bsq[:, (kc % 2) * T:(kc % 2 + 1) * T],
                                      xf[:, kc * T:(kc + 1) * T], AF.Square
                                      ).then_inc(s_sq1, 1)

            def ac_rstd1(j):
                W(scalar, v_var1, j + 1)
                W(scalar, SB3, 16 * (3 * j - 1))
                scalar.activation(rowb[0:1, T:2 * T], rowb[0:1, T:2 * T],
                                  AF.Ln, bias=ep5[:, :], scale=1.0)
                scalar.activation(rowc[0:1, 0:T], rowb[0:1, T:2 * T], AF.Exp,
                                  bias=0.0, scale=-0.5).then_inc(s_rw1, 1)

            def ac_sq2(j):
                for kc in range(KH):
                    W(scalar, v_hln, 8 * j + kc + 1)
                    W(scalar, t_sqc, 24 * j + 8 + kc - 1)
                    scalar.activation(bsq[:, (kc % 2) * T:(kc % 2 + 1) * T],
                                      hln[:, kc * T:(kc + 1) * T], AF.Square
                                      ).then_inc(s_sq2, 1)

            def ac_rstd2(j):
                W(scalar, v_var2, j + 1)
                W(scalar, SB1, 16 * (3 * j + 2))
                scalar.activation(rowb[0:1, T:2 * T], rowb[0:1, T:2 * T],
                                  AF.Ln, bias=ep6[:, :], scale=1.0)
                scalar.activation(rowc[0:1, 0:T], rowb[0:1, T:2 * T], AF.Exp,
                                  bias=0.0, scale=-0.5).then_inc(s_rw2, 1)

            def ac_conv(j):
                if ph >= 4:
                    W(scalar, v_v, 16 * j)
                for dg in range(KD):
                    W(scalar, t_win, 32 * j + dg + 1)
                    W(scalar, v_cnv, 16 * j + dg - 1)
                    x0 = (dg % 2) * (T + 3)
                    scalar.copy(xss[:, x0 + 3:x0 + T + 3], pmm[dg % 4][:, :])
                    scalar.copy(xss[:, x0:x0 + 3], halo[:, dg * 3:dg * 3 + 3])
                    scalar.copy(halo[:, dg * 3:dg * 3 + 3],
                                xss[:, x0 + T:x0 + T + 3]).then_inc(s_ev, 1)
                    W(scalar, v_cnv, 16 * j + dg + 1)
                    scalar.activation(xcv[:, dg * T:(dg + 1) * T],
                                      cacc[:, (dg % 2) * T:(dg % 2 + 1) * T],
                                      AF.Silu, bias=cbc[:, dg:dg + 1],
                                      scale=1.0).then_inc(s_cs, 1)

            def ac_zsilu1(j, dg):
                if dg == 0 and ph >= 5:
                    W(scalar, t_wo, 8 * j)
                W(scalar, t_win, 32 * j + 16 + dg + 1)
                scalar.activation(sz[:, dg * T:(dg + 1) * T],
                                  pmm[3][:, :], AF.Silu
                                  ).then_inc(s_zs, 1)

            def ac_bcevac(j):
                W(scalar, t_bc, j + 1)
                scalar.copy(bc16[:, :], pbc[:, :]).then_inc(v_bc16, 1)

            def ac_sp(j):
                for dg in range(KD):
                    W(scalar, t_dlt, 16 * j + dg + 1)
                    if dg >= 1:
                        W(scalar, v_dlt, 16 * j + dg)
                    asl = scrA[:, (dg % 2) * T:(dg % 2 + 1) * T]
                    scalar.activation(asl, pmm[dg % 2][:, :], AF.Abs)
                    scalar.activation(asl, asl, AF.Exp, bias=0.0, scale=-1.0)
                    scalar.activation(asl, asl, AF.Ln, bias=onf[:, :],
                                      scale=1.0)
                    scalar.activation(scrB[:, :], pmm[dg % 2][:, :], AF.Relu
                                      ).then_inc(s_sp, 1)
                    W(scalar, v_dlt, 16 * j + dg + 1)
                    dsl = dlt[:, (dg % 2) * T:(dg % 2 + 1) * T]
                    for q in range(NSC):
                        W(scalar, v_scan, 8 * KD * j + 8 * dg + q - 1)
                        buf = dAb[(NSC * dg + q) % 2]
                        for jj in range(NQ):
                            n = NQ * q + jj
                            aa = scalar.activation(
                                buf[:, jj * BL + 1: jj * BL + 1 + T],
                                dsl, AF.Exp, bias=0.0,
                                scale=Ang[:, dg * DS + n:dg * DS + n + 1])
                        aa.then_inc(s_dA, 1)
                    if ph >= 2:
                        ac_zsilu1(j, dg)

            def ac_sq3(j):
                for kc in range(KH):
                    W(scalar, v_xnw, 8 * j + kc + 1)
                    W(scalar, t_sqc, 24 * j + 16 + kc - 1)
                    scalar.activation(bsq[:, (kc % 2) * T:(kc % 2 + 1) * T],
                                      xnw[:, kc * T:(kc + 1) * T], AF.Square
                                      ).then_inc(s_sq3, 1)

            def ac_rstd3(j):
                W(scalar, v_var3, j + 1)
                W(scalar, SB2, 16 * (2 * j + 1))
                scalar.activation(rowb[0:1, T:2 * T], rowb[0:1, T:2 * T],
                                  AF.Ln, bias=ep5[:, :], scale=1.0)
                scalar.activation(rowc[0:1, 0:T], rowb[0:1, T:2 * T], AF.Exp,
                                  bias=0.0, scale=-0.5).then_inc(s_rw3, 1)

            def ac_habs(j):
                for kc in range(KH):
                    W(scalar, v_h2, 8 * j + kc + 1)
                    W(scalar, v_qtr, 8 * j + kc)
                    scalar.activation(hab[:, :],
                                      h2r[:, kc * T:(kc + 1) * T], AF.Abs
                                      ).then_inc(s_hab, 1)

            if ph >= 1:
                ac_sq1(0)
                ac_rstd1(0)
            for i in range(NT):
                if ph < 1:
                    break
                ac_sq2(i)
                ac_rstd2(i)
                if ph >= 2:
                    ac_conv(i)
                if ph >= 3:
                    ac_bcevac(i)
                if ph >= 4:
                    ac_sp(i)
                elif ph >= 2:
                    for dg in range(KD):
                        ac_zsilu1(i, dg)
                if ph >= 6:
                    ac_sq3(i)
                    ac_rstd3(i)
                if i + 1 < NT:
                    ac_sq1(i + 1)
                    ac_rstd1(i + 1)
                if ph >= 6:
                    ac_habs(i)

    return nc


_NC_CACHE = {}


def kernel(**inputs):
    x = np.asarray(inputs["L0_attn_x"])
    B = x.shape[0]
    if "nc" not in _NC_CACHE:
        _NC_CACHE["nc"] = build_nc()
    nc = _NC_CACHE["nc"]

    common = {
        "w_in": np.ascontiguousarray(inputs["L0_attn_w_in"], np.float32),
        "w_xp": np.ascontiguousarray(inputs["L0_attn_w_xproj"], np.float32),
        "w_ou": np.ascontiguousarray(inputs["L0_attn_w_out"], np.float32),
        "wmod": np.ascontiguousarray(inputs["L0_wmod"], np.float32),
        "alog": np.ascontiguousarray(inputs["L0_attn_A_log"], np.float32),
        "dprm": np.ascontiguousarray(inputs["L0_attn_D_param"], np.float32),
        "cnvw": np.ascontiguousarray(
            np.asarray(inputs["L0_attn_conv_w"]).reshape(DI, 4), np.float32),
        "cnvb": np.ascontiguousarray(inputs["L0_attn_conv_b"], np.float32),
        "nrmw": np.ascontiguousarray(inputs["L0_attn_norm_w"], np.float32),
        "fbia": np.ascontiguousarray(inputs["L0_ffn_bias"], np.float32),
        "wsca": np.ascontiguousarray(
            np.asarray(inputs["L0_ffn_weight_scale"]).reshape(1, 1),
            np.float32),
        "pack": np.ascontiguousarray(
            np.asarray(inputs["L0_ffn_packed_weight"]).reshape(256, 1024),
            np.int32),
        "idbf": np.eye(128, dtype=ml_dtypes.bfloat16),
        "onbf": np.ones((128, 1), dtype=ml_dtypes.bfloat16),
        "onf3": np.ones((128, 1), dtype=np.float32),
    }
    cmat = np.asarray(inputs["L0_c"], np.float32)
    in_maps = []
    for b in range(B):
        m = dict(common)
        m["xin"] = np.ascontiguousarray(x[b].T, np.float32)
        m["crow"] = np.ascontiguousarray(cmat[b], np.float32)
        in_maps.append(m)

    res = run_bass_kernel_spmd(nc, in_maps, core_ids=list(range(NCORE)))
    outs = [np.asarray(r["out"]).T for r in res.results]
    return np.stack(outs, axis=0)


if __name__ == "__main__":
    build_nc()
    print("build ok")


# version 07
# speedup vs baseline: 1.0861x; 1.0367x over previous
import contextlib
import os
import numpy as np
import ml_dtypes
import concourse.bass as bass
import concourse.mybir as mybir
from concourse.bass_utils import run_bass_kernel_spmd

F32 = mybir.dt.float32
BF16 = mybir.dt.bfloat16
I32 = mybir.dt.int32
AF = mybir.ActivationFunctionType
ALU = mybir.AluOpType
AX = mybir.AxisListType

H = 1024
L = 8192
DI = 2048
DS = 16
T = 512
NT = L // T
KH = H // 128
KD = DI // 128
NQ = 2
NSC = DS // NQ
BL = T + 1
RC = 12582912.0
NCORE = 8


def build_nc():
    ph = int(os.environ.get("KPH", "7"))
    su = int(os.environ.get("KSU", "4"))
    nc = bass.Bass("TRN2", target_bir_lowering=False,
                   detect_race_conditions=False)

    xin = nc.dram_tensor("xin", [H, L], F32, kind="ExternalInput")
    w_in = nc.dram_tensor("w_in", [H, 2 * DI], F32, kind="ExternalInput")
    w_xp = nc.dram_tensor("w_xp", [DI, 2 * DS + DI], F32, kind="ExternalInput")
    w_ou = nc.dram_tensor("w_ou", [DI, H], F32, kind="ExternalInput")
    wmod = nc.dram_tensor("wmod", [H, 6 * H], F32, kind="ExternalInput")
    alog = nc.dram_tensor("alog", [DI, DS], F32, kind="ExternalInput")
    dprm = nc.dram_tensor("dprm", [DI], F32, kind="ExternalInput")
    cnvw = nc.dram_tensor("cnvw", [DI, 4], F32, kind="ExternalInput")
    cnvb = nc.dram_tensor("cnvb", [DI], F32, kind="ExternalInput")
    nrmw = nc.dram_tensor("nrmw", [H], F32, kind="ExternalInput")
    fbia = nc.dram_tensor("fbia", [H], F32, kind="ExternalInput")
    crow = nc.dram_tensor("crow", [H], F32, kind="ExternalInput")
    wsca = nc.dram_tensor("wsca", [1, 1], F32, kind="ExternalInput")
    pack = nc.dram_tensor("pack", [256, 1024], I32, kind="ExternalInput")
    idbf = nc.dram_tensor("idbf", [128, 128], BF16, kind="ExternalInput")
    onbf = nc.dram_tensor("onbf", [128, 1], BF16, kind="ExternalInput")
    onf3 = nc.dram_tensor("onf3", [128, 1], F32, kind="ExternalInput")
    out = nc.dram_tensor("out", [H, L], F32, kind="ExternalOutput")
    dbg = nc.dram_tensor("dbg", [6, 128, KD * T], F32, kind="ExternalOutput")
    dbh = nc.dram_tensor("dbh", [6, 128, KD * T], BF16, kind="ExternalOutput")

    wi_s = nc.dram_tensor("wi_s", [8 * KH * 128 * 512], BF16)
    wxbc_s = nc.dram_tensor("wxbc_s", [KD * 128 * 32], BF16)
    wxd_s = nc.dram_tensor("wxd_s", [16 * KD * 128 * 128], BF16)
    wo_s = nc.dram_tensor("wo_s", [8 * KD * 128 * 128], BF16)
    wf_s = nc.dram_tensor("wf_s", [4 * KH * 128 * 256], BF16)
    rows1 = nc.dram_tensor("rows1", [2, T], BF16)
    rows2 = nc.dram_tensor("rows2", [1, T], BF16)
    rows3 = nc.dram_tensor("rows3", [2, T], BF16)
    bcscr = nc.dram_tensor("bcscr", [2 * DS, T], BF16)
    qsscr = nc.dram_tensor("qsscr", [T], F32)
    qiscr = nc.dram_tensor("qiscr", [T], F32)

    es = contextlib.ExitStack()
    with es:
        def sb(name, shape, dt=BF16):
            return es.enter_context(nc.sbuf_tensor(name, shape, dt))

        def psum(name, shape, dt=F32):
            return es.enter_context(nc.psum_tensor(name, shape, dt))

        xf = sb("xf", [128, KH * T], F32)
        bsq = sb("bsq", [128, 2 * T])
        hln = sb("hln", [128, KH * T])
        h1 = sb("h1", [128, KH * T])
        xss = sb("xss", [128, 2 * (T + 3)])
        halo = sb("halo", [128, KD * 3])
        cacc = sb("cacc", [128, 2 * T])
        gtmp = sb("gtmp", [128, T])
        xcv = sb("xcv", [128, KD * T])
        sz = sb("sz", [128, KD * T])
        scrA = sb("scrA", [128, 2 * T], F32)
        scrB = sb("scrB", [128, T], F32)
        dlt = sb("dlt", [128, 2 * T])
        dxs = sb("dxs", [128, 2 * T])
        Brep = sb("Brep", [128, DS * T])
        Crep = sb("Crep", [128, DS * T])
        bc16 = sb("bc16", [32, T])
        dAb = [sb(f"dA{j}", [128, NQ * BL]) for j in range(2)]
        ub = sb("ub", [128, NQ * BL])
        hb = sb("hb", [128, NQ * BL])
        yac = sb("yac", [128, T], F32)
        stt = sb("stt", [128, KD * DS], F32)
        vsc = sb("vsc", [128, T], F32)
        xnw = sb("xnw", [128, KH * T])
        h2r = sb("h2r", [128, KH * T])
        hab = sb("hab", [128, T])
        qmx = sb("qmx", [128, T])
        qcl = sb("qcl", [128, 4], F32)
        r1bc = sb("r1bc", [128, 2 * T])
        r2bc = sb("r2bc", [128, T])
        r3bc = sb("r3bc", [128, 2 * T])
        sbc = sb("sbc", [128, T], F32)
        wibc = sb("wibc", [128, T], F32)
        rowb = sb("rowb", [128, 2 * T], F32)
        rowc = sb("rowc", [128, 2 * T])
        xo = sb("xo", [128, 2 * T], F32)
        wiu = sb("wiu", [128, 2 * 4096])
        wxu = sb("wxu", [128, 2 * 2048])
        wbcu = sb("wbcu", [128, KD * 32])
        wou = sb("wou", [128, 2 * 2048])
        wfu = sb("wfu", [128, 2 * 2048])
        Ang = sb("Ang", [128, KD * DS], F32)
        Dcl = sb("Dcl", [128, KD], F32)
        cwc = sb("cwc", [128, KD * 4], F32)
        cbc = sb("cbc", [128, KD], F32)
        nwc = sb("nwc", [128, KH], F32)
        fbc = sb("fbc", [128, KH], F32)
        ca1 = sb("ca1", [128, KH], F32)
        sav = sb("sav", [128, KH], F32)
        gav = sb("gav", [128, KH], F32)
        cb1 = sb("cb1", [128, KH], F32)
        sbv = sb("sbv", [128, KH], F32)
        gbv = sb("gbv", [128, KH], F32)
        gbq = sb("gbq", [128, KH], F32)
        fb2 = sb("fb2", [128, KH], F32)
        wsc = sb("wsc", [128, 1], F32)
        crw = sb("crw", [128, KH], F32)
        slc = sb("slc", [128, KH], F32)
        ep5 = sb("ep5", [1, 1], F32)
        ep6 = sb("ep6", [1, 1], F32)
        idb = sb("idb", [128, 128])
        onb = sb("onb", [128, 1])
        onf = sb("onf", [128, 1], F32)

        psA = psum("psA", [128, T])
        psB = psum("psB", [128, T])
        pmm = [psum(f"pmm{j}", [128, T]) for j in range(4)]
        pbc = psum("pbc", [32, T])
        ptp = psum("ptp", [128, 256], BF16)

        def sem(name):
            return es.enter_context(nc.semaphore(name))

        SX = sem("SX"); SWI = sem("SWI"); SWX = sem("SWX"); SWO = sem("SWO")
        SWF = sem("SWF"); SB1 = sem("SB1"); SB2 = sem("SB2"); SBC = sem("SBC")
        SB3 = sem("SB3"); SBQ = sem("SBQ"); SOUT = sem("SOUT")
        SCST = sem("SCST"); SWLD = sem("SWLD"); SWST = sem("SWST")
        SWML = sem("SWML"); SWFS = sem("SWFS")
        t_st1 = sem("t_st1"); t_st2 = sem("t_st2"); t_st3 = sem("t_st3")
        t_win = sem("t_win"); t_wig = sem("t_wig"); t_bc = sem("t_bc")
        t_dlt = sem("t_dlt"); t_wxg = sem("t_wxg"); t_wo = sem("t_wo")
        t_wog = sem("t_wog"); t_ffn = sem("t_ffn"); t_qtp = sem("t_qtp")
        t_sqc = sem("t_sqc"); TSU1 = sem("TSU1"); TSU2 = sem("TSU2")
        v_var1 = sem("v_var1"); v_rw1 = sem("v_rw1"); v_hln = sem("v_hln")
        v_var2 = sem("v_var2"); v_h1 = sem("v_h1"); v_cnv = sem("v_cnv")
        v_bc16 = sem("v_bc16"); v_dlt = sem("v_dlt"); v_scan = sem("v_scan")
        v_v = sem("v_v"); v_xnw = sem("v_xnw"); v_var3 = sem("v_var3")
        v_rw3 = sem("v_rw3"); v_h2 = sem("v_h2"); v_qtr = sem("v_qtr")
        v_mxm = sem("v_mxm"); v_qrd = sem("v_qrd"); v_qrw = sem("v_qrw")
        v_r = sem("v_r"); v_out = sem("v_out")
        VSU1 = sem("VSU1"); VSU2 = sem("VSU2"); VSU3 = sem("VSU3")
        s_sq1 = sem("s_sq1"); s_rw1 = sem("s_rw1"); s_ev = sem("s_ev")
        s_cs = sem("s_cs"); s_zs = sem("s_zs"); s_sq2 = sem("s_sq2")
        s_rw2 = sem("s_rw2"); s_sp = sem("s_sp"); s_dA = sem("s_dA")
        s_rw3 = sem("s_rw3"); s_hab = sem("s_hab"); s_sq3 = sem("s_sq3")
        SSU1 = sem("SSU1"); SSU2 = sem("SSU2"); SSU3 = sem("SSU3")

        cast_slices = [("wi", k, 4096) for k in range(KH)] + \
                      [("wx", k, 2080) for k in range(KD)] + \
                      [("wo", k, 1024) for k in range(KD)]
        NSL = len(cast_slices)
        stores_per = [2 if kind == "wx" else 1 for kind, _, _ in cast_slices]
        stores_thru = np.cumsum([0] + stores_per)
        castA = xf[:, 0:4096]
        castB = xcv[:, :].bitcast(F32)
        cslot = [castA, castB]
        N_CST = 11

        def W(eng, s, v):
            if v > 0:
                eng.wait_ge(s, v)

        block = es.enter_context(nc.Block())

        @block.sync
        def _(sync):
            cst = [0]
            def dcst(dst, src):
                sync.dma_start(dst, src, allow_slow_non_contiguous=True
                               ).then_inc(SCST, 16)
                cst[0] += 1
            dcst(Ang[:, :].rearrange("p (g n) -> p g n", g=KD),
                 alog[:, :].rearrange("(g p) n -> p g n", p=128))
            dcst(Dcl[:, :], dprm[:].rearrange("(g p) -> p g", p=128))
            dcst(cwc[:, :].rearrange("p (g k) -> p g k", g=KD),
                 cnvw[:, :].rearrange("(g p) k -> p g k", p=128))
            dcst(cbc[:, :], cnvb[:].rearrange("(g p) -> p g", p=128))
            dcst(nwc[:, :], nrmw[:].rearrange("(g p) -> p g", p=128))
            dcst(fbc[:, :], fbia[:].rearrange("(g p) -> p g", p=128))
            dcst(idb[:, :], idbf[:, :])
            dcst(onb[:, :], onbf[:, :])
            dcst(onf[:, :], onf3[:, :])
            dcst(wsc[:, :], wsca[:, :].broadcast_to([128, 1]))
            dcst(crw[:, :], crow[:].rearrange("(g p) -> p g", p=128))
            dcst(scrA[:, 0:1024].bitcast(I32), pack[0:128, :])
            dcst(hln[:, :].bitcast(I32)[:, 0:1024], pack[128:256, :])
            assert cst[0] == N_CST + 2
            cast_list = cast_slices if su >= 2 else []
            for si, (kind, k, wid) in enumerate(cast_list):
                sl = si % 2
                if si >= 2:
                    W(sync, VSU1, si - 1)
                src = {"wi": w_in, "wx": w_xp, "wo": w_ou}[kind]
                sync.dma_start(cslot[sl][:, 0:wid],
                               src[k * 128:(k + 1) * 128, :]).then_inc(SWLD, 16)
                W(sync, VSU1, si + 1)
                cap = wiu[:, sl * 4096: sl * 4096 + wid]
                if kind == "wi":
                    sync.dma_start(
                        bass.AP(wi_s, k * 128 * 512,
                                [[512, 128], [KH * 128 * 512, 8], [1, 512]]),
                        cap.rearrange("p (g m) -> p g m", g=8),
                    ).then_inc(SWST, 16)
                elif kind == "wx":
                    sync.dma_start(
                        bass.AP(wxbc_s, k * 128 * 32, [[32, 128], [1, 32]]),
                        cap[:, 0:32],
                    ).then_inc(SWST, 16)
                    sync.dma_start(
                        bass.AP(wxd_s, k * 128 * 128,
                                [[128, 128], [KD * 128 * 128, 16], [1, 128]]),
                        cap[:, 32:2080].rearrange("p (g m) -> p g m", g=16),
                    ).then_inc(SWST, 16)
                else:
                    sync.dma_start(
                        bass.AP(wo_s, k * 128 * 128,
                                [[128, 128], [KD * 128 * 128, 8], [1, 128]]),
                        cap.rearrange("p (g m) -> p g m", g=8),
                    ).then_inc(SWST, 16)
            if su >= 2:
                W(sync, VSU1, NSL)
            for mc in range(48 if su >= 3 else 0):
                if mc >= 2:
                    W(sync, TSU1, mc - 1)
                sync.dma_start(
                    cslot[mc % 2][:, 0:1024].rearrange("p (g m) -> p g m", g=KH),
                    bass.AP(wmod, mc * 128,
                            [[6 * H, 128], [128 * 6 * H, KH], [1, 128]]),
                ).then_inc(SWML, 16)
            if su >= 4:
                W(sync, SSU2, 64)
            for u in range(4 if su >= 4 else 0):
                sync.dma_start(
                    bass.AP(wf_s, u * KH * 128 * 256,
                            [[256, 128], [128 * 256, KH], [1, 256]]),
                    Brep[:, :].rearrange("p (kc o) -> p kc o", kc=KH)
                    [:, :, u * 256:(u + 1) * 256],
                ).then_inc(SWFS, 16)

            if su >= 3:
                W(sync, TSU1, 48)
            if su >= 2:
                W(sync, SWST, 16 * int(stores_thru[NSL]))
            def sy_x(j):
                if ph >= 5:
                    W(sync, v_xnw, 8 * j)
                else:
                    W(sync, v_hln, 8 * j)
                for kc in range(KH):
                    sync.dma_start(
                        xf[:, kc * T:(kc + 1) * T],
                        xin[kc * 128:(kc + 1) * 128, j * T:(j + 1) * T],
                    ).then_inc(SX, 16)

            def sy_rows1(j):
                W(sync, v_rw1, j + 1)
                sync.dma_start(rows1[0:1, :], rowc[0:1, 0:T]).then_inc(SB1, 16)
                sync.dma_start(rows1[1:2, :], rowc[0:1, T:2 * T]
                               ).then_inc(SB1, 16)
                W(sync, SB1, 16 * (3 * j + 2))
                sync.dma_start(
                    r1bc[:, :].rearrange("p (a t) -> p a t", a=2),
                    rows1[:, :][None, :, :].broadcast_to([128, 2, T]),
                ).then_inc(SB1, 16)

            def sy_rows2(j):
                W(sync, s_rw2, j + 1)
                sync.dma_start(rows2[:, :], rowc[0:1, 0:T]).then_inc(SB2, 16)
                W(sync, SB2, 16 * (2 * j + 1))
                sync.dma_start(
                    r2bc[:, :], rows2[0, :][None, :].broadcast_to([128, T]),
                ).then_inc(SB2, 16)

            def sy_rows3(j):
                W(sync, v_rw3, j + 1)
                sync.dma_start(rows3[0:1, :], rowc[0:1, 0:T]).then_inc(SB3, 16)
                sync.dma_start(rows3[1:2, :], rowc[0:1, T:2 * T]
                               ).then_inc(SB3, 16)
                W(sync, SB3, 16 * (3 * j + 2))
                sync.dma_start(
                    r3bc[:, :].rearrange("p (a t) -> p a t", a=2),
                    rows3[:, :][None, :, :].broadcast_to([128, 2, T]),
                ).then_inc(SB3, 16)

            def sy_quant(j):
                W(sync, v_qrw, j + 1)
                sync.dma_start(bass.AP(qsscr, 0, [[1, 128], [128, 4]]),
                               qcl[:, 0:4], allow_slow_non_contiguous=True
                               ).then_inc(SBQ, 16)
                W(sync, SBQ, 16 * (2 * j + 1))
                sync.dma_start(
                    wibc[:, :], qsscr[:][None, :].broadcast_to([128, T]),
                ).then_inc(SBQ, 16)

            def sy_win(j, us=range(8)):
                for u in us:
                    if u >= 2 or j > 0:
                        W(sync, t_win, 32 * j + 4 * u - 4)
                    sync.dma_start(
                        wiu[:, (u % 2) * 4096:(u % 2) * 4096 + 4096]
                        .rearrange("p (g m) -> p g m", g=8),
                        bass.AP(wi_s, u * KH * 128 * 512,
                                [[512, 128], [128 * 512, 8], [1, 512]]),
                    ).then_inc(SWI, 16)

            def sy_bcw(j):
                W(sync, t_bc, j)
                sync.dma_start(
                    wbcu[:, :].rearrange("p (g m) -> p g m", g=KD),
                    bass.AP(wxbc_s, 0, [[32, 128], [128 * 32, 16], [1, 32]]),
                ).then_inc(SWX, 16)

            def sy_bcb(j):
                W(sync, v_bc16, j + 1)
                sync.dma_start(bcscr[:, :], bc16[:, :]).then_inc(SBC, 16)
                W(sync, SBC, 16 * (5 * j + 1))
                if j == 0:
                    W(sync, SWFS, 64)
                if ph >= 4:
                    W(sync, v_v, 16 * j)
                HS = DS // 2
                for hf in range(2):
                    sync.dma_start(
                        Brep[:, hf * HS * T:(hf + 1) * HS * T]
                        .rearrange("p (n t) -> p n t", n=HS),
                        bcscr[hf * HS:(hf + 1) * HS, :][None, :, :]
                        .broadcast_to([128, HS, T]),
                    ).then_inc(SBC, 16)
                    sync.dma_start(
                        Crep[:, hf * HS * T:(hf + 1) * HS * T]
                        .rearrange("p (n t) -> p n t", n=HS),
                        bcscr[DS + hf * HS:DS + (hf + 1) * HS, :][None, :, :]
                        .broadcast_to([128, HS, T]),
                    ).then_inc(SBC, 16)

            def sy_dltw(j, us=range(16)):
                for u in us:
                    W(sync, t_dlt, 16 * j + u - 1)
                    sync.dma_start(
                        wxu[:, (u % 2) * 2048:(u % 2) * 2048 + 2048]
                        .rearrange("p (g m) -> p g m", g=KD),
                        bass.AP(wxd_s, u * KD * 128 * 128,
                                [[128, 128], [128 * 128, 16], [1, 128]]),
                    ).then_inc(SWX, 16)

            def sy_wow(j):
                for u in range(8):
                    W(sync, t_wo, 8 * j + u - 1)
                    sync.dma_start(
                        wou[:, (u % 2) * 2048:(u % 2) * 2048 + 2048]
                        .rearrange("p (g m) -> p g m", g=KD),
                        bass.AP(wo_s, u * KD * 128 * 128,
                                [[128, 128], [128 * 128, 16], [1, 128]]),
                    ).then_inc(SWO, 16)

            def sy_ffnw(j, us):
                for u in us:
                    W(sync, t_ffn, 8 * j + 2 * u - 2)
                    sync.dma_start(
                        wfu[:, (u % 2) * 2048:(u % 2) * 2048 + 2048]
                        .rearrange("p (g m) -> p g m", g=KH),
                        bass.AP(wf_s, u * KH * 128 * 256,
                                [[256, 128], [128 * 256, 8], [1, 256]]),
                    ).then_inc(SWF, 16)

            def sy_ffnout(j):
                for mc in range(KH):
                    if mc % 2 == 0 and mc >= 4:
                        sy_ffnw(j, [mc // 2])
                    W(sync, v_out, 8 * j + mc + 1)
                    sync.dma_start(
                        out[mc * 128:(mc + 1) * 128, j * T:(j + 1) * T],
                        xo[:, (mc % 2) * T:(mc % 2 + 1) * T],
                    ).then_inc(SOUT, 16)

            if ph >= 1:
                sy_x(0)
                sy_rows1(0)
                sy_rows2(0)
            if ph >= 2:
                sy_win(0, range(6))
            if ph >= 3:
                sy_bcw(0)
                sy_dltw(0, range(2))
                sy_bcb(0)
                sy_dltw(0, range(2, 6))
            if ph >= 2:
                sy_win(0, range(6, 7))
            if ph >= 3:
                sy_dltw(0, range(6, 10))
            if ph >= 2:
                sy_win(0, range(7, 8))
            if ph >= 3:
                sy_dltw(0, range(10, 16))
            if ph >= 5:
                sy_wow(0)
            for i in range(NT):
                if ph < 1:
                    break
                if ph >= 7:
                    sy_ffnw(i, range(2))
                if ph >= 6:
                    sy_rows3(i)
                if i + 1 < NT:
                    sy_x(i + 1)
                    sy_rows1(i + 1)
                if ph >= 6:
                    sy_quant(i)
                if i + 1 < NT and ph >= 2:
                    sy_win(i + 1, range(2))
                if ph >= 7:
                    sy_ffnout(i)
                if i + 1 < NT:
                    sy_rows2(i + 1)
                if i + 1 < NT:
                    if ph >= 2:
                        sy_win(i + 1, range(2, 6))
                    if ph >= 3:
                        sy_bcw(i + 1)
                        sy_dltw(i + 1, range(2))
                        sy_bcb(i + 1)
                        sy_dltw(i + 1, range(2, 6))
                    if ph >= 2:
                        sy_win(i + 1, range(6, 7))
                    if ph >= 3:
                        sy_dltw(i + 1, range(6, 10))
                    if ph >= 2:
                        sy_win(i + 1, range(7, 8))
                    if ph >= 3:
                        sy_dltw(i + 1, range(10, 16))
                    if ph >= 5:
                        sy_wow(i + 1)
            if ph >= 7:
                sync.wait_ge(SOUT, 16 * 8 * NT)
                if int(os.environ.get("KDBG", "0")):
                    W(sync, v_var1, NT + 1)
                    nd = [0]
                    def dmp(j, ap):
                        sync.dma_start(dbg[j, :, 0:ap.shape[1]], ap
                                       ).then_inc(SCST, 16)
                        nd[0] += 1
                    dmp(0, stt[:, :])
                    dmp(1, scrB[:, :])
                    dmp(2, yac[:, :])
                    dmp(3, vsc[:, :])
                    for gi, gap in enumerate((sav, ca1, gav, sbv, cb1,
                                              gbv, fb2)):
                        sync.dma_start(dbg[4, :, gi * 8:gi * 8 + 8],
                                       gap[:, :]).then_inc(SCST, 16)
                        nd[0] += 1
                    def dmh(j, ap):
                        sync.dma_start(dbh[j, :, 0:ap.shape[1]], ap
                                       ).then_inc(SCST, 16)
                        nd[0] += 1
                    dmh(0, hln[:, :])
                    dmh(1, h1[:, :])
                    dmh(2, xcv[:, :])
                    dmh(3, sz[:, :])
                    dmh(4, xnw[:, :])
                    dmh(5, h2r[:, :])
                    sync.wait_ge(SCST, 16 * (N_CST + 2 + nd[0]))
            elif ph == 6:
                sync.wait_ge(v_r, 8 * NT)
            elif ph == 5:
                sync.wait_ge(v_xnw, 8 * NT)
            elif ph == 4:
                sync.wait_ge(v_v, 16 * NT)
            elif ph == 3:
                sync.wait_ge(t_dlt, 16 * NT)
            elif ph == 2:
                sync.wait_ge(s_zs, 16 * NT)
            elif ph == 1:
                sync.wait_ge(v_h1, 8 * NT)
            else:
                if su >= 3:
                    sync.wait_ge(VSU3, 1)
                if su >= 4:
                    sync.wait_ge(SSU2, 64)
                    sync.wait_ge(SWFS, 64)
                if su == 2:
                    sync.wait_ge(VSU1, NSL)
                sync.wait_ge(SSU3, 1)

        @block.tensor
        def _(tensor):
            if su >= 3:
                W(tensor, SSU1, 1)
            for mc in range(48 if su >= 3 else 0):
                W(tensor, SWML, 16 * (mc + 1))
                for k in range(KH):
                    mm = tensor.matmul(
                        psA[:, mc:mc + 1],
                        cslot[mc % 2][:, k * 128:(k + 1) * 128],
                        slc[:, k:k + 1],
                        start=(k == 0), stop=(k == KH - 1))
                mm.then_inc(TSU1, 1)
            if su >= 4:
                W(tensor, VSU2, 1)
            for j in range(64 if su >= 4 else 0):
                W(tensor, SSU2, j)
                tensor.transpose(
                    ptp[:, (j % 2) * 128:(j % 2) * 128 + 128],
                    sz[:, j * 128:(j + 1) * 128],
                    idb[:, :],
                ).then_inc(TSU2, 1)

            W(tensor, SCST, 16 * (N_CST + 2))
            W(tensor, VSU3, 1)

            def pe_stats1(j):
                if ph >= 6:
                    W(tensor, v_var3, j)
                for kc in range(KH):
                    W(tensor, SX, 16 * (8 * j + kc + 1))
                    tensor.matmul(psA[0:1, :], onf[:, :],
                                  xf[:, kc * T:(kc + 1) * T],
                                  start=(kc == 0), stop=(kc == KH - 1))
                    W(tensor, s_sq1, 8 * j + kc + 1)
                    mm = tensor.matmul(psB[0:1, :], onb[:, :],
                                       bsq[:, (kc % 2) * T:(kc % 2) * T + T],
                                       start=(kc == 0), stop=(kc == KH - 1))
                    mm.then_inc(t_sqc, 1)

            def pe_stats2(j):
                W(tensor, v_var1, j + 1)
                for kc in range(KH):
                    W(tensor, s_sq2, 8 * j + kc + 1)
                    mm = tensor.matmul(psA[0:1, :], onb[:, :],
                                       bsq[:, (kc % 2) * T:(kc % 2) * T + T],
                                       start=(kc == 0), stop=(kc == KH - 1))
                    mm.then_inc(t_sqc, 1)

            def pe_win(j):
                W(tensor, v_h1, 8 * (j + 1))
                for g in range(4):
                    W(tensor, SWI, 16 * (8 * j + g + 1))
                    for ml in range(4):
                        mc = 4 * g + ml
                        W(tensor, s_ev, 16 * j + mc - 3)
                        for kc in range(KH):
                            mm = tensor.matmul(
                                pmm[ml][:, :],
                                wiu[:, (g % 2) * 4096 + kc * 512 + ml * 128:
                                    (g % 2) * 4096 + kc * 512 + (ml + 1) * 128],
                                h1[:, kc * T:(kc + 1) * T],
                                start=(kc == 0), stop=(kc == KH - 1))
                        mm.then_inc(t_win, 1)


            def pe_bc(j):
                W(tensor, SWX, 16 * (17 * j + 3))
                W(tensor, v_bc16, j)
                if ph >= 7:
                    W(tensor, v_out, 8 * j)
                for kc in range(KD):
                    W(tensor, s_cs, 16 * j + kc + 1)
                    mmb = tensor.matmul(pbc[:, :],
                                        wbcu[:, kc * 32:(kc + 1) * 32],
                                        xcv[:, kc * T:(kc + 1) * T],
                                        start=(kc == 0), stop=(kc == KD - 1))
                    for dg in range(2):
                        mm = tensor.matmul(
                            pmm[dg][:, :],
                            wxu[:, dg * 2048 + kc * 128:
                                dg * 2048 + (kc + 1) * 128],
                            xcv[:, kc * T:(kc + 1) * T],
                            start=(kc == 0), stop=(kc == KD - 1))
                        if kc == KD - 1:
                            mm.then_inc(t_dlt, 1)
                mmb.then_inc(t_bc, 1)

            def pe_zmm(j, dg):
                g = dg // 4
                if dg % 4 == 0:
                    W(tensor, SWI, 16 * (8 * j + 4 + g + 1))
                if dg == 0:
                    W(tensor, v_xnw, 8 * j)
                else:
                    W(tensor, s_zs, 16 * j + dg)
                ml = dg % 4
                for kc in range(KH):
                    mm = tensor.matmul(
                        pmm[3][:, :],
                        wiu[:, ((4 + g) % 2) * 4096 + kc * 512 + ml * 128:
                            ((4 + g) % 2) * 4096 + kc * 512 + (ml + 1) * 128],
                        h1[:, kc * T:(kc + 1) * T],
                        start=(kc == 0), stop=(kc == KH - 1))
                mm.then_inc(t_win, 1)

            def pe_delta(j):
                for dg in range(2):
                    if ph >= 2:
                        pe_zmm(j, dg)
                for dg in range(2, KD):
                    W(tensor, SWX, 16 * (17 * j + dg + 2))
                    if ph >= 4:
                        W(tensor, s_sp, 16 * j + dg - 1)
                    for kc in range(KD):
                        mm = tensor.matmul(
                            pmm[dg % 2][:, :],
                            wxu[:, (dg % 2) * 2048 + kc * 128:
                                (dg % 2) * 2048 + (kc + 1) * 128],
                            xcv[:, kc * T:(kc + 1) * T],
                            start=(kc == 0), stop=(kc == KD - 1))
                    mm.then_inc(t_dlt, 1)
                    if ph >= 2:
                        pe_zmm(j, dg)

            def pe_wout(j):
                W(tensor, v_v, 16 * (j + 1))
                for mc in range(KH):
                    W(tensor, SWO, 16 * (8 * j + mc + 1))
                    W(tensor, v_xnw, 8 * j + mc - 1)
                    for kc in range(KD):
                        mm = tensor.matmul(
                            pmm[2 + mc % 2][:, :],
                            wou[:, (mc % 2) * 2048 + kc * 128:
                                (mc % 2) * 2048 + (kc + 1) * 128],
                            sz[:, kc * T:(kc + 1) * T],
                            start=(kc == 0), stop=(kc == KD - 1))
                    mm.then_inc(t_wo, 1)

            def pe_ln2(j):
                W(tensor, v_xnw, 8 * (j + 1))
                W(tensor, v_var2, j + 1)
                for kc in range(KH):
                    tensor.matmul(psA[0:1, :], onb[:, :],
                                  xnw[:, kc * T:(kc + 1) * T],
                                  start=(kc == 0), stop=(kc == KH - 1))
                for kc in range(KH):
                    W(tensor, s_sq3, 8 * j + kc + 1)
                    mm = tensor.matmul(psB[0:1, :], onb[:, :],
                                       bsq[:, (kc % 2) * T:(kc % 2) * T + T],
                                       start=(kc == 0), stop=(kc == KH - 1))
                    mm.then_inc(t_sqc, 1)

            def pe_qtp(j):
                W(tensor, v_qtr, 8 * (j + 1))
                for jj in range(4):
                    W(tensor, v_qrd, 4 * j + jj)
                    tensor.transpose(ptp[:, (jj % 2) * 128:(jj % 2) * 128 + 128],
                                     qmx[:, jj * 128:(jj + 1) * 128],
                                     idb[:, :]).then_inc(t_qtp, 1)

            def pe_ffn(j):
                W(tensor, v_r, 8 * (j + 1))
                for mc in range(KH):
                    u = mc // 2
                    W(tensor, SWF, 16 * (4 * j + u + 1))
                    W(tensor, v_out, 8 * j + mc - 1)
                    for kc in range(KH):
                        mm = tensor.matmul(
                            pmm[mc % 2][:, :],
                            wfu[:, (u % 2) * 2048 + kc * 256 + (mc % 2) * 128:
                                (u % 2) * 2048 + kc * 256 + (mc % 2) * 128 + 128],
                            h2r[:, kc * T:(kc + 1) * T],
                            start=(kc == 0), stop=(kc == KH - 1))
                    mm.then_inc(t_ffn, 1)

            if ph >= 1:
                pe_stats1(0)
            if ph >= 1:
                pe_stats2(0)
            for i in range(NT):
                if ph < 1:
                    break
                if ph >= 2:
                    pe_win(i)
                if ph >= 3:
                    pe_bc(i)
                    pe_delta(i)
                else:
                    for dg in range(KD if ph >= 2 else 0):
                        pe_zmm(i, dg)
                if ph >= 5:
                    pe_wout(i)
                if ph >= 6:
                    pe_ln2(i)
                if i + 1 < NT:
                    pe_stats1(i + 1)
                if ph >= 6:
                    pe_qtp(i)
                if i + 1 < NT:
                    pe_stats2(i + 1)
                if ph >= 7:
                    pe_ffn(i)

        @block.vector
        def _(vector):
            vector.memset(stt[:, :], 0.0)
            vector.memset(halo[:, :], 0.0)
            vector.memset(ep5[:, :], 1e-5)
            vector.memset(ep6[:, :], 1e-6)
            for j in range(2):
                for q in range(NQ):
                    vector.memset(dAb[j][:, q * BL:q * BL + 1], 0.0)
            for si, (kind, k, wid) in enumerate(cast_slices if su >= 2 else []):
                sl = si % 2
                W(vector, SWLD, 16 * (si + 1))
                if si >= 2:
                    W(vector, SWST, 16 * int(stores_thru[si - 1]))
                vector.tensor_copy(wiu[:, sl * 4096: sl * 4096 + wid],
                                   cslot[sl][:, 0:wid]).then_inc(VSU1, 1)
            if su < 4:
                vector.engine_nop().then_inc(VSU2, 1)
            else:
                W(vector, SCST, 16 * (N_CST + 2))
                ti = bsq[:, :].bitcast(I32)
                for s4 in range(4):
                    for ot in range(2):
                        srcw = (scrA[:, 0:1024].bitcast(I32) if ot == 0
                                else hln[:, :].bitcast(I32)[:, 0:1024])
                        for hf in range(2):
                            sw = srcw[:, hf * 512:(hf + 1) * 512]
                            if s4 == 0:
                                vector.tensor_scalar(ti[:, 0:512], sw, 3, None,
                                                     ALU.bitwise_and)
                            else:
                                vector.tensor_scalar(ti[:, 0:512], sw, 2 * s4,
                                                     None, ALU.logical_shift_right)
                                vector.tensor_scalar(ti[:, 0:512], ti[:, 0:512],
                                                     3, None, ALU.bitwise_and)
                            vector.tensor_scalar(ti[:, 0:512], ti[:, 0:512], 1,
                                                 None, ALU.subtract)
                            vector.tensor_copy(
                                sz[:, (s4 * 2 + ot) * 1024 + hf * 512:
                                   (s4 * 2 + ot) * 1024 + (hf + 1) * 512],
                                ti[:, 0:512])
                vector.engine_nop().then_inc(VSU2, 1)
            if su < 3:
                vector.engine_nop().then_inc(VSU3, 1)
            else:
                W(vector, TSU1, 48)
                vector.tensor_copy(sav[:, :], psA[:, 0:8])
                vector.tensor_scalar(ca1[:, :], psA[:, 8:16], 1.0, None, ALU.add)
                vector.tensor_copy(sbv[:, :], psA[:, 24:32])
                vector.tensor_scalar(cb1[:, :], psA[:, 32:40], 1.0, None, ALU.add)
                g = scrA[:, 0 * 64:0 * 64 + 48]
                a = scrA[:, 1 * 64:1 * 64 + 48]
                b = scrA[:, 2 * 64:2 * 64 + 48]
                m = scrA[:, 3 * 64:3 * 64 + 48]
                d = scrA[:, 4 * 64:4 * 64 + 48]
                e = scrA[:, 5 * 64:5 * 64 + 48]
                r = scrA[:, 6 * 64:6 * 64 + 48]
                vector.tensor_copy(g, psA[:, 0:48])
                vector.tensor_scalar(a, g, 0.25, None, ALU.max)
                vector.tensor_scalar(b, g, -0.25, None, ALU.min)
                vector.tensor_scalar(m, g, 0.0, None, ALU.is_ge)
                vector.tensor_tensor(d, a, b, ALU.subtract)
                vector.tensor_tensor(e, m, d, ALU.mult)
                vector.tensor_tensor(r, b, e, ALU.add)
                vector.tensor_copy(gav[:, :], r[:, 16:24])
                vector.tensor_copy(gbv[:, :], r[:, 40:48])
                vector.tensor_scalar(gbq[:, :], r[:, 40:48], wsc[:, 0:1],
                                     1.0 / 127.0, ALU.mult, ALU.mult)
                vector.tensor_tensor(fb2[:, :], fbc[:, :], r[:, 40:48],
                                     ALU.mult).then_inc(VSU3, 1)

            def dv_rows1(j):
                W(vector, t_sqc, 24 * j + 8)
                W(vector, s_rw3, j)
                vector.tensor_scalar(rowb[0:1, 0:T], psA[0:1, :], 1.0 / H,
                                     None, ALU.mult)
                vector.tensor_tensor(rowb[0:1, T:2 * T], rowb[0:1, 0:T],
                                     rowb[0:1, 0:T], ALU.mult)
                vector.scalar_tensor_tensor(rowb[0:1, T:2 * T], psB[0:1, :],
                                            1.0 / H, rowb[0:1, T:2 * T],
                                            ALU.mult, ALU.subtract
                                            ).then_inc(v_var1, 1)
                W(vector, s_rw1, j + 1)
                W(vector, SB3, 16 * (3 * j - 1))
                vector.scalar_tensor_tensor(rowc[0:1, T:2 * T],
                                            rowb[0:1, 0:T], -1.0,
                                            rowc[0:1, 0:T],
                                            ALU.mult, ALU.mult
                                            ).then_inc(v_rw1, 1)

            def dv_apply1(j):
                W(vector, SB1, 16 * (3 * j + 3))
                for kc in range(KH):
                    sA = scrA[:, (kc % 2) * T:(kc % 2 + 1) * T]
                    vector.tensor_tensor(sA, xf[:, kc * T:(kc + 1) * T],
                                         r1bc[:, 0:T], ALU.mult)
                    vector.tensor_tensor(sA, sA, r1bc[:, T:2 * T], ALU.add)
                    vector.tensor_scalar(hln[:, kc * T:(kc + 1) * T], sA,
                                         ca1[:, kc:kc + 1], sav[:, kc:kc + 1],
                                         ALU.mult, ALU.add).then_inc(v_hln, 1)

            def dv_rmsrow(j):
                W(vector, t_sqc, 24 * j + 16)
                vector.tensor_scalar(rowb[0:1, T:2 * T], psA[0:1, :], 1.0 / H,
                                     None, ALU.mult).then_inc(v_var2, 1)

            def dv_h1(j):
                W(vector, SB2, 16 * (2 * j + 2))
                if ph >= 2:
                    W(vector, t_win, 32 * j)
                for kc in range(KH):
                    vector.scalar_tensor_tensor(
                        h1[:, kc * T:(kc + 1) * T],
                        hln[:, kc * T:(kc + 1) * T],
                        nwc[:, kc:kc + 1], r2bc[:, :],
                        ALU.mult, ALU.mult).then_inc(v_h1, 1)

            def dv_conv(j):
                for dg in range(KD):
                    W(vector, s_ev, 16 * j + dg + 1)
                    W(vector, s_cs, 16 * j + dg - 1)
                    x0 = (dg % 2) * (T + 3)
                    ca = cacc[:, (dg % 2) * T:(dg % 2 + 1) * T]
                    vector.tensor_scalar(ca, xss[:, x0:x0 + T],
                                         cwc[:, dg * 4:dg * 4 + 1], None,
                                         ALU.mult)
                    for kk in range(1, 3):
                        vector.scalar_tensor_tensor(
                            ca, xss[:, x0 + kk:x0 + kk + T],
                            cwc[:, dg * 4 + kk:dg * 4 + kk + 1], ca,
                            ALU.mult, ALU.add)
                    vector.scalar_tensor_tensor(
                        ca, xss[:, x0 + 3:x0 + 3 + T],
                        cwc[:, dg * 4 + 3:dg * 4 + 4], ca,
                        ALU.mult, ALU.add).then_inc(v_cnv, 1)

            def dv_scan(j):
                for dg in range(KD):
                    W(vector, s_sp, 16 * j + dg + 1)
                    if dg >= 2:
                        W(vector, s_dA, 8 * KD * j + 8 * dg - 8)
                    dsl = dlt[:, (dg % 2) * T:(dg % 2 + 1) * T]
                    vector.tensor_tensor(dsl, scrA[:, (dg % 2) * T:
                                                    (dg % 2 + 1) * T],
                                         scrB[:, :], ALU.add)
                    vector.tensor_tensor(dxs[:, (dg % 2) * T:
                                              (dg % 2 + 1) * T], dsl,
                                         xcv[:, dg * T:(dg + 1) * T], ALU.mult
                                         ).then_inc(v_dlt, 1)
                    for q in range(NSC):
                        if dg == 0 and q == 0:
                            W(vector, SBC, 16 * (5 * j + 3))
                        if dg == 0 and q == NSC // 2:
                            W(vector, SBC, 16 * (5 * j + 5))
                        st0 = dg * DS + NQ * q
                        ubv = ub[:, :].rearrange("p (q b) -> p q b", q=NQ)
                        hbv = hb[:, :].rearrange("p (q b) -> p q b", q=NQ)
                        for blk in range(NQ):
                            vector.tensor_copy(
                                ub[:, blk * BL:blk * BL + 1],
                                stt[:, st0 + blk:st0 + blk + 1])
                        vector.tensor_tensor(
                            ubv[:, :, 1:BL],
                            dxs[:, (dg % 2) * T:(dg % 2 + 1) * T]
                            [:, None, :].broadcast_to([128, NQ, T]),
                            Brep[:, :].rearrange("p (n t) -> p n t", n=DS)
                            [:, NQ * q:NQ * q + NQ, :],
                            ALU.mult)
                        W(vector, s_dA, 8 * KD * j + 8 * dg + q + 1)
                        vector.tensor_tensor_scan(
                            hb[:, :], dAb[(NSC * dg + q) % 2][:, :], ub[:, :],
                            0.0, ALU.mult, ALU.add).then_inc(v_scan, 1)
                        for blk in range(NQ):
                            vector.tensor_copy(
                                stt[:, st0 + blk:st0 + blk + 1],
                                hb[:, blk * BL + BL - 1:blk * BL + BL])
                        vector.tensor_tensor(
                            ubv[:, :, 1:BL], hbv[:, :, 1:BL],
                            Crep[:, :].rearrange("p (n t) -> p n t", n=DS)
                            [:, NQ * q:NQ * q + NQ, :],
                            ALU.mult)
                        if q == 0:
                            vector.tensor_tensor(yac[:, :], ubv[:, 0, 1:BL],
                                                 ubv[:, 1, 1:BL], ALU.add)
                        else:
                            vector.tensor_tensor(yac[:, :], yac[:, :],
                                                 ubv[:, 0, 1:BL], ALU.add)
                            vector.tensor_tensor(yac[:, :], yac[:, :],
                                                 ubv[:, 1, 1:BL], ALU.add)
                    W(vector, s_zs, 16 * j + dg + 1)
                    vector.scalar_tensor_tensor(
                        vsc[:, :], xcv[:, dg * T:(dg + 1) * T],
                        Dcl[:, dg:dg + 1], yac[:, :], ALU.mult, ALU.add)
                    vector.tensor_tensor(sz[:, dg * T:(dg + 1) * T],
                                         vsc[:, :], sz[:, dg * T:(dg + 1) * T],
                                         ALU.mult).then_inc(v_v, 1)

            def dv_wout(j):
                for mc in range(KH):
                    W(vector, t_wo, 8 * j + mc + 1)
                    esl = scrA[:, (mc % 2) * T:(mc % 2 + 1) * T]
                    vector.tensor_tensor(esl, pmm[2 + mc % 2][:, :],
                                         hln[:, mc * T:(mc + 1) * T], ALU.add)
                    vector.scalar_tensor_tensor(
                        xnw[:, mc * T:(mc + 1) * T], esl,
                        gav[:, mc:mc + 1], xf[:, mc * T:(mc + 1) * T],
                        ALU.mult, ALU.add).then_inc(v_xnw, 1)

            def dv_rows2(j):
                W(vector, t_sqc, 24 * j + 24)
                vector.tensor_scalar(rowb[0:1, 0:T], psA[0:1, :], 1.0 / H,
                                     None, ALU.mult)
                vector.tensor_tensor(rowb[0:1, T:2 * T], rowb[0:1, 0:T],
                                     rowb[0:1, 0:T], ALU.mult)
                vector.scalar_tensor_tensor(rowb[0:1, T:2 * T], psB[0:1, :],
                                            1.0 / H, rowb[0:1, T:2 * T],
                                            ALU.mult, ALU.subtract
                                            ).then_inc(v_var3, 1)
                W(vector, s_rw3, j + 1)
                vector.scalar_tensor_tensor(rowc[0:1, T:2 * T],
                                            rowb[0:1, 0:T], -1.0,
                                            rowc[0:1, 0:T],
                                            ALU.mult, ALU.mult
                                            ).then_inc(v_rw3, 1)

            def dv_h2(j):
                W(vector, SB3, 16 * (3 * j + 3))
                for kc in range(KH):
                    sA = scrA[:, (kc % 2) * T:(kc % 2 + 1) * T]
                    vector.tensor_tensor(sA, xnw[:, kc * T:(kc + 1) * T],
                                         r3bc[:, 0:T], ALU.mult)
                    vector.tensor_tensor(sA, sA, r3bc[:, T:2 * T], ALU.add)
                    vector.tensor_scalar(h2r[:, kc * T:(kc + 1) * T], sA,
                                         cb1[:, kc:kc + 1], sbv[:, kc:kc + 1],
                                         ALU.mult, ALU.add).then_inc(v_h2, 1)

            def dv_qmx(j):
                W(vector, t_qtp, 4 * j)
                for kc in range(KH):
                    W(vector, s_hab, 8 * j + kc + 1)
                    if kc == 0:
                        vector.tensor_copy(qmx[:, :], hab[:, :]
                                           ).then_inc(v_qtr, 1)
                    else:
                        vector.tensor_tensor(qmx[:, :], qmx[:, :],
                                             hab[:, :], ALU.max
                                             ).then_inc(v_qtr, 1)

            def dv_qred(j):
                for jj in range(4):
                    W(vector, t_qtp, 4 * j + jj + 1)
                    vector.reduce_max(qcl[:, jj:jj + 1],
                                      ptp[:, (jj % 2) * 128:(jj % 2) * 128 + 128],
                                      AX.X).then_inc(v_qrd, 1)
                vector.engine_nop().then_inc(v_qrw, 1)

            def dv_scale_r(j):
                W(vector, SBQ, 16 * (2 * j + 2))
                vector.tensor_scalar(wibc[:, :], wibc[:, :], 1e-5, None,
                                     ALU.max)
                vector.reciprocal(sbc[:, :], wibc[:, :])
                vector.tensor_scalar(sbc[:, :], sbc[:, :], 127.0, None,
                                     ALU.mult)
                for kc in range(KH):
                    sA = scrA[:, (kc % 2) * T:(kc % 2 + 1) * T]
                    vector.tensor_tensor(sA, h2r[:, kc * T:(kc + 1) * T],
                                         sbc[:, :], ALU.mult)
                    vector.tensor_scalar(h2r[:, kc * T:(kc + 1) * T], sA,
                                         RC, RC, ALU.add, ALU.subtract
                                         ).then_inc(v_r, 1)

            def dv_ffnout(j):
                for mc in range(KH):
                    W(vector, t_ffn, 8 * j + mc + 1)
                    sA = scrA[:, (mc % 2) * T:(mc % 2 + 1) * T]
                    vector.tensor_tensor(sA, pmm[mc % 2][:, :], wibc[:, :],
                                         ALU.mult)
                    vector.tensor_scalar(scrB[:, :], sA,
                                         gbq[:, mc:mc + 1], fb2[:, mc:mc + 1],
                                         ALU.mult, ALU.add)
                    W(vector, SOUT, 16 * (8 * j + mc - 1))
                    vector.tensor_tensor(xo[:, (mc % 2) * T:(mc % 2 + 1) * T],
                                         scrB[:, :],
                                         xnw[:, mc * T:(mc + 1) * T], ALU.add
                                         ).then_inc(v_out, 1)

            if ph >= 1:
                dv_rows1(0)
                dv_apply1(0)
                dv_rmsrow(0)
                dv_h1(0)
            if ph >= 2:
                dv_conv(0)
            for i in range(NT):
                if ph < 1:
                    break
                if ph >= 4:
                    dv_scan(i)
                if ph >= 5:
                    dv_wout(i)
                if ph >= 6:
                    dv_rows2(i)
                if i + 1 < NT:
                    dv_rows1(i + 1)
                if ph >= 6:
                    dv_h2(i)
                    dv_qmx(i)
                    dv_qred(i)
                if i + 1 < NT:
                    dv_apply1(i + 1)
                if ph >= 6:
                    dv_scale_r(i)
                if i + 1 < NT:
                    dv_rmsrow(i + 1)
                if ph >= 7:
                    dv_ffnout(i)
                if i + 1 < NT:
                    dv_h1(i + 1)
                    if ph >= 2:
                        dv_conv(i + 1)

            if int(os.environ.get("KDBG", "0")) and ph >= 7:
                W(vector, SOUT, 16 * 8 * NT)
                vector.tensor_copy(scrB[:, :], pmm[2][:, :])
                vector.tensor_copy(vsc[:, :], pmm[3][:, :]).then_inc(v_var1, 1)

        @block.scalar
        def _(scalar):
            W(scalar, SCST, 16 * (N_CST + 2))
            scalar.activation(slc[:, :], crw[:, :], AF.Silu
                              ).then_inc(SSU1, 1)
            scalar.activation(Ang[:, :], Ang[:, :], AF.Exp)
            scalar.activation(Ang[:, :], Ang[:, :], AF.Copy, bias=0.0,
                              scale=-1.0).then_inc(SSU3, 1)
            for j in range(64 if su >= 4 else 0):
                W(scalar, TSU2, j + 1)
                s4, rem = divmod(j, 2 * KH)
                ot, ib = divmod(rem, KH)
                o0 = s4 * 256 + ot * 128
                scalar.copy(Brep[:, ib * H + o0: ib * H + o0 + 128],
                            ptp[:, (j % 2) * 128:(j % 2) * 128 + 128]
                            ).then_inc(SSU2, 1)

            def ac_sq1(j):
                for kc in range(KH):
                    W(scalar, SX, 16 * (8 * j + kc + 1))
                    W(scalar, t_sqc, 24 * j + kc - 1)
                    scalar.activation(bsq[:, (kc % 2) * T:(kc % 2 + 1) * T],
                                      xf[:, kc * T:(kc + 1) * T], AF.Square
                                      ).then_inc(s_sq1, 1)

            def ac_rstd1(j):
                W(scalar, v_var1, j + 1)
                W(scalar, SB3, 16 * (3 * j - 1))
                scalar.activation(rowb[0:1, T:2 * T], rowb[0:1, T:2 * T],
                                  AF.Ln, bias=ep5[:, :], scale=1.0)
                scalar.activation(rowc[0:1, 0:T], rowb[0:1, T:2 * T], AF.Exp,
                                  bias=0.0, scale=-0.5).then_inc(s_rw1, 1)

            def ac_sq2(j):
                for kc in range(KH):
                    W(scalar, v_hln, 8 * j + kc + 1)
                    W(scalar, t_sqc, 24 * j + 8 + kc - 1)
                    scalar.activation(bsq[:, (kc % 2) * T:(kc % 2 + 1) * T],
                                      hln[:, kc * T:(kc + 1) * T], AF.Square
                                      ).then_inc(s_sq2, 1)

            def ac_rstd2(j):
                W(scalar, v_var2, j + 1)
                W(scalar, SB1, 16 * (3 * j + 2))
                scalar.activation(rowb[0:1, T:2 * T], rowb[0:1, T:2 * T],
                                  AF.Ln, bias=ep6[:, :], scale=1.0)
                scalar.activation(rowc[0:1, 0:T], rowb[0:1, T:2 * T], AF.Exp,
                                  bias=0.0, scale=-0.5).then_inc(s_rw2, 1)

            def ac_conv(j):
                if ph >= 4:
                    W(scalar, v_v, 16 * j)
                for dg in range(KD):
                    W(scalar, t_win, 32 * j + dg + 1)
                    W(scalar, v_cnv, 16 * j + dg - 1)
                    x0 = (dg % 2) * (T + 3)
                    scalar.copy(xss[:, x0 + 3:x0 + T + 3], pmm[dg % 4][:, :])
                    scalar.copy(xss[:, x0:x0 + 3], halo[:, dg * 3:dg * 3 + 3])
                    scalar.copy(halo[:, dg * 3:dg * 3 + 3],
                                xss[:, x0 + T:x0 + T + 3]).then_inc(s_ev, 1)
                    W(scalar, v_cnv, 16 * j + dg + 1)
                    scalar.activation(xcv[:, dg * T:(dg + 1) * T],
                                      cacc[:, (dg % 2) * T:(dg % 2 + 1) * T],
                                      AF.Silu, bias=cbc[:, dg:dg + 1],
                                      scale=1.0).then_inc(s_cs, 1)

            def ac_zsilu1(j, dg):
                if dg == 0 and ph >= 5:
                    W(scalar, t_wo, 8 * j)
                W(scalar, t_win, 32 * j + 16 + dg + 1)
                scalar.activation(sz[:, dg * T:(dg + 1) * T],
                                  pmm[3][:, :], AF.Silu
                                  ).then_inc(s_zs, 1)

            def ac_bcevac(j):
                W(scalar, t_bc, j + 1)
                scalar.copy(bc16[:, :], pbc[:, :]).then_inc(v_bc16, 1)

            def ac_sp(j):
                for dg in range(KD):
                    W(scalar, t_dlt, 16 * j + dg + 1)
                    if dg >= 1:
                        W(scalar, v_dlt, 16 * j + dg)
                    asl = scrA[:, (dg % 2) * T:(dg % 2 + 1) * T]
                    scalar.activation(asl, pmm[dg % 2][:, :], AF.Abs)
                    scalar.activation(asl, asl, AF.Exp, bias=0.0, scale=-1.0)
                    scalar.activation(asl, asl, AF.Ln, bias=onf[:, :],
                                      scale=1.0)
                    scalar.activation(scrB[:, :], pmm[dg % 2][:, :], AF.Relu
                                      ).then_inc(s_sp, 1)
                    W(scalar, v_dlt, 16 * j + dg + 1)
                    dsl = dlt[:, (dg % 2) * T:(dg % 2 + 1) * T]
                    for q in range(NSC):
                        W(scalar, v_scan, 8 * KD * j + 8 * dg + q - 1)
                        buf = dAb[(NSC * dg + q) % 2]
                        for jj in range(NQ):
                            n = NQ * q + jj
                            aa = scalar.activation(
                                buf[:, jj * BL + 1: jj * BL + 1 + T],
                                dsl, AF.Exp, bias=0.0,
                                scale=Ang[:, dg * DS + n:dg * DS + n + 1])
                        aa.then_inc(s_dA, 1)
                        if q == 2 and ph >= 2:
                            ac_zsilu1(j, dg)

            def ac_sq3(j):
                for kc in range(KH):
                    W(scalar, v_xnw, 8 * j + kc + 1)
                    W(scalar, t_sqc, 24 * j + 16 + kc - 1)
                    scalar.activation(bsq[:, (kc % 2) * T:(kc % 2 + 1) * T],
                                      xnw[:, kc * T:(kc + 1) * T], AF.Square
                                      ).then_inc(s_sq3, 1)

            def ac_rstd3(j):
                W(scalar, v_var3, j + 1)
                W(scalar, SB2, 16 * (2 * j + 1))
                scalar.activation(rowb[0:1, T:2 * T], rowb[0:1, T:2 * T],
                                  AF.Ln, bias=ep5[:, :], scale=1.0)
                scalar.activation(rowc[0:1, 0:T], rowb[0:1, T:2 * T], AF.Exp,
                                  bias=0.0, scale=-0.5).then_inc(s_rw3, 1)

            def ac_habs(j):
                for kc in range(KH):
                    W(scalar, v_h2, 8 * j + kc + 1)
                    W(scalar, v_qtr, 8 * j + kc)
                    scalar.activation(hab[:, :],
                                      h2r[:, kc * T:(kc + 1) * T], AF.Abs
                                      ).then_inc(s_hab, 1)

            if ph >= 1:
                ac_sq1(0)
                ac_rstd1(0)
            for i in range(NT):
                if ph < 1:
                    break
                ac_sq2(i)
                ac_rstd2(i)
                if ph >= 2:
                    ac_conv(i)
                if ph >= 3:
                    ac_bcevac(i)
                if ph >= 4:
                    ac_sp(i)
                elif ph >= 2:
                    for dg in range(KD):
                        ac_zsilu1(i, dg)
                if ph >= 6:
                    ac_sq3(i)
                    ac_rstd3(i)
                if i + 1 < NT:
                    ac_sq1(i + 1)
                    ac_rstd1(i + 1)
                if ph >= 6:
                    ac_habs(i)

    return nc


_NC_CACHE = {}


def kernel(**inputs):
    x = np.asarray(inputs["L0_attn_x"])
    B = x.shape[0]
    if "nc" not in _NC_CACHE:
        _NC_CACHE["nc"] = build_nc()
    nc = _NC_CACHE["nc"]

    common = {
        "w_in": np.ascontiguousarray(inputs["L0_attn_w_in"], np.float32),
        "w_xp": np.ascontiguousarray(inputs["L0_attn_w_xproj"], np.float32),
        "w_ou": np.ascontiguousarray(inputs["L0_attn_w_out"], np.float32),
        "wmod": np.ascontiguousarray(inputs["L0_wmod"], np.float32),
        "alog": np.ascontiguousarray(inputs["L0_attn_A_log"], np.float32),
        "dprm": np.ascontiguousarray(inputs["L0_attn_D_param"], np.float32),
        "cnvw": np.ascontiguousarray(
            np.asarray(inputs["L0_attn_conv_w"]).reshape(DI, 4), np.float32),
        "cnvb": np.ascontiguousarray(inputs["L0_attn_conv_b"], np.float32),
        "nrmw": np.ascontiguousarray(inputs["L0_attn_norm_w"], np.float32),
        "fbia": np.ascontiguousarray(inputs["L0_ffn_bias"], np.float32),
        "wsca": np.ascontiguousarray(
            np.asarray(inputs["L0_ffn_weight_scale"]).reshape(1, 1),
            np.float32),
        "pack": np.ascontiguousarray(
            np.asarray(inputs["L0_ffn_packed_weight"]).reshape(256, 1024),
            np.int32),
        "idbf": np.eye(128, dtype=ml_dtypes.bfloat16),
        "onbf": np.ones((128, 1), dtype=ml_dtypes.bfloat16),
        "onf3": np.ones((128, 1), dtype=np.float32),
    }
    cmat = np.asarray(inputs["L0_c"], np.float32)
    in_maps = []
    for b in range(B):
        m = dict(common)
        m["xin"] = np.ascontiguousarray(x[b].T, np.float32)
        m["crow"] = np.ascontiguousarray(cmat[b], np.float32)
        in_maps.append(m)

    res = run_bass_kernel_spmd(nc, in_maps, core_ids=list(range(NCORE)))
    outs = [np.asarray(r["out"]).T for r in res.results]
    return np.stack(outs, axis=0)


if __name__ == "__main__":
    build_nc()
    print("build ok")
